# version 08
# baseline (speedup 1.0000x reference)
import math
import numpy as np
import ml_dtypes
from contextlib import ExitStack

import concourse.bass as bass
import concourse.mybir as mybir
from concourse.bass_utils import run_bass_kernel_spmd

F32 = mybir.dt.float32
BF16 = mybir.dt.bfloat16
I32 = mybir.dt.int32
I16 = mybir.dt.int16
AF = mybir.ActivationFunctionType
ALU = mybir.AluOpType
AX = mybir.AxisListType
BF = ml_dtypes.bfloat16

T = 2048
TC = 16
D = 1024
DC = 8
HID = 2816
HC = 22
STRIPS = [(0, 512), (512, 512), (1024, 512), (1536, 512), (2048, 512), (2560, 256)]
NSUP = 4
LAM0 = float(0.8 - 0.6 * math.exp(-0.3 * 12))
SCAL = 1.0 / math.sqrt(32.0)

N_PHASES = 3
DEBUG_DUMP = False
DEBUG_SEL = "ffn,proj,epi,nrm"
ATTN_STAGE = 9
ATTN_NTAU = 4
ATTN_NG = 8
ATTN_NS = 16
ATTN_SUB = 3


class Sem:
    def __init__(self, ctx, nc, name):
        self.h = ctx.enter_context(nc.semaphore(name))
        self.n = 0

    def inc(self, inst, k=1):
        inst.then_inc(self.h, k)
        self.n += k
        return self.n


def build_program(n_phases=3):
    nc = bass.Bass("TRN2", target_bir_lowering=False)

    x_d = nc.dram_tensor("x", [T, D], F32, kind="ExternalInput").ap()
    wq_d = nc.dram_tensor("wq", [D, D], F32, kind="ExternalInput").ap()
    wk_d = nc.dram_tensor("wk", [D, D], F32, kind="ExternalInput").ap()
    wv_d = nc.dram_tensor("wv", [D, D], F32, kind="ExternalInput").ap()
    wo_d = nc.dram_tensor("wo", [D, D], F32, kind="ExternalInput").ap()
    wg_d = nc.dram_tensor("wg", [HID, D], F32, kind="ExternalInput").ap()
    wu_d = nc.dram_tensor("wu", [HID, D], F32, kind="ExternalInput").ap()
    wd_d = nc.dram_tensor("wd", [D, HID], F32, kind="ExternalInput").ap()
    lam_d = nc.dram_tensor("lam4", [1, 128], F32, kind="ExternalInput").ap()
    subln_d = nc.dram_tensor("subln", [64, 1], F32, kind="ExternalInput").ap()
    idn_d = nc.dram_tensor("idn", [128, 128], BF16, kind="ExternalInput").ap()
    ones_d = nc.dram_tensor("onesrow", [128, 128], F32, kind="ExternalInput").ap()
    out_d = nc.dram_tensor("out", [T, D], F32, kind="ExternalOutput").ap()
    dbg = {}
    if DEBUG_DUMP:
        dbg["ln"] = nc.dram_tensor("dbg_ln", [128, D], F32, kind="ExternalOutput").ap()
        dbg["xq"] = nc.dram_tensor("dbg_xq", [128, D], BF16, kind="ExternalOutput").ap()
        dbg["gate"] = nc.dram_tensor("dbg_gate", [128, 512], F32, kind="ExternalOutput").ap()
        dbg["up"] = nc.dram_tensor("dbg_up", [128, 512], F32, kind="ExternalOutput").ap()
        dbg["h"] = nc.dram_tensor("dbg_h", [128, 512], F32, kind="ExternalOutput").ap()
        dbg["hq"] = nc.dram_tensor("dbg_hq", [128, 512], BF16, kind="ExternalOutput").ap()
        dbg["wdq0"] = nc.dram_tensor("dbg_wdq0", [128, D], BF16, kind="ExternalOutput").ap()
        dbg["lamcol"] = nc.dram_tensor("dbg_lamcol", [128, 1], F32, kind="ExternalOutput").ap()
        dbg["rowscr"] = nc.dram_tensor("dbg_rowscr", [128, 1024], F32, kind="ExternalOutput").ap()
        dbg["scro"] = nc.dram_tensor("dbg_scro", [128, 512], F32, kind="ExternalOutput").ap()
        dbg["nrm0"] = nc.dram_tensor("dbg_nrm0", [128, 512], BF16, kind="ExternalOutput").ap()
        dbg["e0"] = nc.dram_tensor("dbg_e0", [128, 512], BF16, kind="ExternalOutput").ap()
        dbg["qt0"] = nc.dram_tensor("dbg_qt0", [128, 512], BF16, kind="ExternalOutput").ap()
        dbg["kt0"] = nc.dram_tensor("dbg_kt0", [128, 2048], BF16, kind="ExternalOutput").ap()
        dbg["v0"] = nc.dram_tensor("dbg_v0", [128, 1040], BF16, kind="ExternalOutput").ap()
        dbg["nrmall"] = nc.dram_tensor("dbg_nrmall", [128, 16, 512], BF16, kind="ExternalOutput").ap()
        dbg["ln2t0"] = nc.dram_tensor("dbg_ln2t0", [128, 512], BF16, kind="ExternalOutput").ap()
    wgq_s = nc.dram_tensor("wgq_s", [DC, 128, HID], BF16, kind="Internal").ap()
    wuq_s = nc.dram_tensor("wuq_s", [DC, 128, HID], BF16, kind="Internal").ap()
    wdq_s = nc.dram_tensor("wdq_s", [HC, 128, D], BF16, kind="Internal").ap()

    ctx = ExitStack()
    with ctx:
        ent = ctx.enter_context
        xres = ent(nc.sbuf_tensor("xres", [128, TC, D], F32))
        idn = ent(nc.sbuf_tensor("idn_sb", [128, 128], BF16))
        ones = ent(nc.sbuf_tensor("ones_sb", [128, 128], F32))
        lamcol = ent(nc.sbuf_tensor("lamcol", [128, 1], F32))
        sublncol = ent(nc.sbuf_tensor("sublncol", [128, 1], F32))
        stats = ent(nc.sbuf_tensor("stats", [128, 18], F32))
        maskp = ent(nc.sbuf_tensor("maskp", [128, 1], I32))
        masks = ent(nc.sbuf_tensor("masks", [128, 1], I32))
        two16 = ent(nc.sbuf_tensor("two16", [128, 1], I16))
        eps5col = ent(nc.sbuf_tensor("eps5col", [128, 1], F32))
        bigscr = ent(nc.sbuf_tensor("bigscr", [128, 1024], F32))
        rowscr = ent(nc.sbuf_tensor("rowscr", [128, 1024], F32))
        Q = {}

        block = ent(nc.Block())

        class BB:
            pass
        b = BB()
        for name in ["s_ld", "s_cast", "s_cast1", "s_sp", "s_pe", "s_act", "s_dve",
                     "s_gp", "s_st", "s_bar", "s_lw", "s_lst", "s_lwd"]:
            setattr(b, name, Sem(ctx, nc, name))

        st = {"qdve": 0, "qgp": 0, "qact_t2": 0, "wld": 0, "pend": None,
              "slot": 0, "wd_ev": 0, "gu_ev": 0}

        def barrier():
            toks = [(b.s_pe.h, b.s_pe.n), (b.s_act.h, b.s_act.n),
                    (b.s_dve.h, b.s_dve.n), (b.s_gp.h, b.s_gp.n)]
            base = b.s_bar.n
            for sec in [block.sync, block.tensor, block.vector, block.scalar,
                        block.gpsimd]:
                def _f(eng):
                    for h, n in toks:
                        if n:
                            eng.wait_ge(h, n)
                    b.s_bar.inc(eng.nop(), 1)
                sec(_f)
            target = base + 5
            for sec in [block.sync, block.tensor, block.vector, block.scalar,
                        block.gpsimd]:
                sec(lambda eng: eng.wait_ge(b.s_bar.h, target))

        def ph0_sync(s):
            for t in range(TC):
                b.s_ld.inc(s.dma_start(out=xres[:, t, :],
                                       in_=x_d[128 * t:128 * (t + 1), :]), 16)
            b.s_ld.inc(s.dma_start(out=idn[:], in_=idn_d[:]), 16)
            b.s_ld.inc(s.dma_start(out=ones[:], in_=ones_d[:]), 16)
            b.s_ld.inc(s.dma_start(out=rowscr[0:1, 0:128], in_=lam_d[:, :]), 16)
            b.s_ld.inc(s.dma_start(out=sublncol[0:64, :], in_=subln_d[:, :]), 16)
        block.sync(ph0_sync)
        tok_ph0 = b.s_ld.n

        def ph0_vec(v):
            v.wait_ge(b.s_ld.h, tok_ph0)
            v.memset(maskp[:], 0x7FFFFFFF)
            v.memset(masks[:], -2147483648)
            t = b.s_dve.inc(v.memset(two16[:], 2))
            v.wait_ge(b.s_dve.h, t)
            t = b.s_dve.inc(v.memset(eps5col[:], 1e-5))
            v.wait_ge(b.s_dve.h, t)
            t = b.s_dve.inc(v.tensor_single_scalar(sublncol[0:64, :], sublncol[0:64, :],
                                                   1.0 - LAM0, ALU.mult))
            v.wait_ge(b.s_dve.h, t)
        def ph0_gate(s):
            s.wait_ge(b.s_ld.h, tok_ph0)
        block.sync(ph0_gate)
        barrier()
        block.vector(ph0_vec)

        def flush_q():
            pe = st.get("pend")
            if pe is None:
                return st["qdve"]
            st["pend"] = None
            p, F, src, dst = pe["p"], pe["F"], pe["src"], pe["dst"]
            BL = F // 16
            shq = Q["sh2"][p] if pe["halve"] else Q["sh"][p]
            tok = {}

            def vsec2(v, pe=pe, p=p, F=F, BL=BL, shq=shq, tok=tok, src=src, dst=dst):
                v.wait_ge(b.s_gp.h, pe["gp"])
                dw = pe["dst_war"]() if callable(pe["dst_war"]) else pe["dst_war"]
                if dw is not None:
                    v.wait_ge(dw[0], dw[1])
                xn2 = Q["xn2"][p][:, :F]
                t2 = Q["t2s"][:, :F]
                b1h = Q["b1h"][:, :F]
                v.tensor_single_scalar(t2.bitcast(I32), xn2.bitcast(I32),
                                       maskp[:], ALU.bitwise_and)
                v.tensor_single_scalar(Q["A"][:, :F], t2, 4.0, ALU.min)
                v.tensor_scalar(b1h, t2, 0.5, -2.0, ALU.mult, ALU.add)
                v.tensor_scalar(Q["B"][:, :F], b1h, 0.0, 2.0, ALU.max, ALU.min)
                v.tensor_scalar(b1h, t2, -9.5, 0.0, ALU.add, ALU.max)
                v.tensor_single_scalar(Q["C"][:, :F], b1h, 1.0, ALU.min)
                v.scalar_tensor_tensor(Q["C"][:, :F], Q["C"][:, :F], two16[:],
                                       Q["B"][:, :F], ALU.mult, ALU.add)
                v.scalar_tensor_tensor(Q["B"][:, :F], Q["C"][:, :F], two16[:],
                                       Q["A"][:, :F], ALU.mult, ALU.add)
                shb = shq[:, :BL].unsqueeze(2).broadcast_to([128, BL, 16])
                v.scalar_tensor_tensor(
                    t2.bitcast(I32).rearrange("p (bb k) -> p bb k", k=16),
                    xn2.bitcast(I32).rearrange("p (bb k) -> p bb k", k=16),
                    masks[:], shb.bitcast(I32), ALU.bitwise_and, ALU.bitwise_or)
                tok["q"] = b.s_dve.inc(v.tensor_tensor(dst, Q["B"][:, :F], t2,
                                                       ALU.mult))
            block.vector(vsec2)
            st["qdve"] = tok["q"]
            if pe["post"] is not None:
                pe["post"](tok["q"])
            return tok["q"]

        def emit_quant(src, dst, F, halve=False, dst_war=None, src_ready=None,
                       post=None):
            BL = F // 16
            p = st["slot"] & 1
            st["slot"] += 1
            tok = {}

            def vsec1(v, p=p):
                if src_ready is not None:
                    v.wait_ge(src_ready[0], src_ready[1])
                sv = src.rearrange("p (bb k) -> p bb k", k=16)
                t = b.s_dve.inc(v.tensor_reduce(
                    Q["m"][:, :BL], sv, AX.X, ALU.max, apply_absolute_value=True))
                v.wait_ge(b.s_dve.h, t)
                t = b.s_dve.inc(v.tensor_scalar(
                    Q["sh"][p][:, :BL], Q["m"][:, :BL], 1.0 / 12.0, 5e-13,
                    ALU.mult, ALU.max))
                v.wait_ge(b.s_dve.h, t)
                tok["r2"] = b.s_dve.inc(v.reciprocal(Q["r2"][p][:, :BL],
                                                     Q["sh"][p][:, :BL]))
                if halve:
                    tok["r2"] = b.s_dve.inc(v.tensor_single_scalar(
                        Q["sh2"][p][:, :BL], Q["sh"][p][:, :BL], 0.5, ALU.mult))
                v.wait_ge(b.s_dve.h, tok["r2"])
            block.vector(vsec1)

            def gsec(g, p=p):
                g.wait_ge(b.s_dve.h, tok["r2"])
                if src_ready is not None:
                    g.wait_ge(src_ready[0], src_ready[1])
                if st["qdve"]:
                    g.wait_ge(b.s_dve.h, st["qdve"])
                r2b = Q["r2"][p][:, :BL].unsqueeze(2).broadcast_to([128, BL, 16])
                xn = Q["xn2"][p]
                tok["gp"] = b.s_gp.inc(g.tensor_tensor(
                    xn[:, :F].rearrange("p (bb k) -> p bb k", k=16),
                    src.rearrange("p (bb k) -> p bb k", k=16), r2b, ALU.mult))
            block.gpsimd(gsec)
            st["qgp"] = tok["gp"]
            st["qr2"] = tok["r2"]

            flush_q()
            st["pend"] = dict(p=p, F=F, src=src, dst=dst, halve=halve,
                              dst_war=dst_war, post=post, gp=tok["gp"])
            return tok

        def emit_ln(src, dst, dst_war=None):
            tok = {}

            def vsec(v):
                t = b.s_dve.inc(v.bn_stats(stats[:, 0:6], src[:, 0:512]))
                t = b.s_dve.inc(v.bn_stats(stats[:, 8:14], src[:, 512:1024]))
                v.wait_ge(b.s_dve.h, t)
                t = b.s_dve.inc(v.tensor_copy(stats[:, 6:8], stats[:, 8:10]))
                v.wait_ge(b.s_dve.h, t)
                t = b.s_dve.inc(v.tensor_copy(stats[:, 8:12], stats[:, 10:14]))
                v.wait_ge(b.s_dve.h, t)
                tok["mv"] = b.s_dve.inc(v.bn_aggr(stats[:, 14:16], stats[:, 0:12]))
                v.wait_ge(b.s_dve.h, tok["mv"])
            block.vector(vsec)

            def asec2(sc):
                sc.wait_ge(b.s_dve.h, tok["mv"])
                t = b.s_act.inc(sc.activation(stats[:, 16:17], stats[:, 15:16], AF.Ln,
                                              bias=eps5col[:, 0:1]))
                sc.wait_ge(b.s_act.h, t)
                tok["rstd"] = b.s_act.inc(sc.activation(stats[:, 16:17], stats[:, 16:17],
                                                        AF.Exp, scale=-0.5))
            block.scalar(asec2)

            def vsec2(v):
                v.wait_ge(b.s_act.h, tok["rstd"])
                t = b.s_dve.inc(v.tensor_scalar(stats[:, 17:18], stats[:, 14:15],
                                                stats[:, 16:17], -1.0,
                                                ALU.mult, ALU.mult))
                v.wait_ge(b.s_dve.h, t)
                if st["qgp"]:
                    v.wait_ge(b.s_gp.h, st["qgp"])
                if dst_war is not None:
                    v.wait_ge(dst_war[0], dst_war[1])
                tok["out"] = b.s_dve.inc(v.tensor_scalar(dst, src, stats[:, 16:17],
                                                         stats[:, 17:18],
                                                         ALU.mult, ALU.add))
            block.vector(vsec2)
            return tok["out"]

        def ffn(first):
            sfx = "a" if first else "b"
            fctx = ExitStack()
            with fctx:
                fent = fctx.enter_context
                wld = fent(nc.sbuf_tensor("wld" + sfx, [128, HID], F32))
                Q["xn2"] = [fent(nc.sbuf_tensor(f"qs_xn2{i}" + sfx, [128, 1024], F32))
                            for i in range(2)]
                Q["t2s"] = bigscr
                Q["b1h"] = fent(nc.sbuf_tensor("qs_b1h" + sfx, [128, 1024], F32))
                Q["A"] = fent(nc.sbuf_tensor("qs_A" + sfx, [128, 1024], I16))
                Q["B"] = fent(nc.sbuf_tensor("qs_B" + sfx, [128, 1024], I16))
                Q["C"] = fent(nc.sbuf_tensor("qs_C" + sfx, [128, 1024], I16))
                Q["m"] = fent(nc.sbuf_tensor("qs_m" + sfx, [128, 64], F32))
                Q["sh"] = [fent(nc.sbuf_tensor(f"qs_sh{i}" + sfx, [128, 64], F32))
                           for i in range(2)]
                Q["sh2"] = [fent(nc.sbuf_tensor(f"qs_sh2{i}" + sfx, [128, 64], F32))
                            for i in range(2)]
                Q["r2"] = [fent(nc.sbuf_tensor(f"qs_r2{i}" + sfx, [128, 64], F32))
                           for i in range(2)]
                ln_sb = wld[:, 0:D]
                wq_chunk = fent(nc.sbuf_tensor("wq_chunk" + sfx, [128, 1024], BF16))
                wgstrip = fent(nc.sbuf_tensor("wgstrip" + sfx, [128, DC, 512], BF16))
                wustrip = fent(nc.sbuf_tensor("wustrip" + sfx, [128, DC, 512], BF16))
                wdqT = fent(nc.sbuf_tensor("wdqT" + sfx, [128, HC, D], BF16))
                xqT = fent(nc.sbuf_tensor("xqT" + sfx, [128, DC, 512], BF16))
                hqT = fent(nc.sbuf_tensor("hqT" + sfx, [128, HC, 512], BF16))
                xq_sb = fent(nc.sbuf_tensor("xq_sb" + sfx, [128, D], BF16))
                silu_sb = [fent(nc.sbuf_tensor(f"silu{i}" + sfx, [128, 512], F32)) for i in range(2)]
                h_sb = [fent(nc.sbuf_tensor(f"h{i}" + sfx, [128, 512], F32)) for i in range(2)]
                hq_sb = [fent(nc.sbuf_tensor(f"hq{i}" + sfx, [128, 512], BF16)) for i in range(2)]
                psG = [fent(nc.psum_tensor(f"psG{i}" + sfx, [128, 512], F32)) for i in range(2)]
                psU = [fent(nc.psum_tensor(f"psU{i}" + sfx, [128, 512], F32)) for i in range(2)]
                psD = [fent(nc.psum_tensor(f"psD{i}" + sfx, [128, 512], F32)) for i in range(2)]
                psT = [fent(nc.psum_tensor(f"psT{i}" + sfx, [128, 128], BF16)) for i in range(2)]

                fst = {"tp": 0, "evac0": 0, "evac1": 0, "strip_pe": 0,
                       "sup_pe": 0, "silu_rd0": 0, "silu_rd1": 0,
                       "hpipe0": 0, "hpipe1": 0, "dres0": 0, "dres1": 0,
                       "wq_pe": 0}

                def emit_tp(src, dst, pe_wait=None, evac_war=()):
                    i = fst["tp"] % 2
                    fst["tp"] += 1
                    tk = {}
                    wars = [w for w in (evac_war if isinstance(evac_war, (list, tuple)) and
                                        (not evac_war or isinstance(evac_war[0], (list, tuple)))
                                        else [evac_war]) if w]

                    def tsec(t):
                        if pe_wait is not None:
                            t.wait_ge(pe_wait[0], pe_wait[1])
                        if fst[f"evac{i}"]:
                            t.wait_ge(b.s_dve.h, fst[f"evac{i}"])
                        tk["tp"] = b.s_pe.inc(t.transpose(psT[i][:], src, idn[:]))
                    block.tensor(tsec)

                    def vsec(v):
                        v.wait_ge(b.s_pe.h, tk["tp"])
                        for w in wars:
                            v.wait_ge(w[0], w[1])
                        tk["ev"] = b.s_dve.inc(v.tensor_copy(dst, psT[i][:]))
                    block.vector(vsec)
                    fst[f"evac{i}"] = tk["ev"]
                    return tk["ev"]

                def wq_war():
                    return (b.s_pe.h, fst["wq_pe"]) if fst["wq_pe"] else None

                def quant_w_chunk(dram_rows, nf, dst, halve, post=None):
                    def ssec(s):
                        if st["qgp"]:
                            s.wait_ge(b.s_gp.h, st["qgp"])
                        if st.get("qr2"):
                            s.wait_ge(b.s_dve.h, st["qr2"])
                        st["wld"] = b.s_lw.inc(
                            s.dma_start(out=wld[:, :nf], in_=dram_rows), 16)
                    block.sync(ssec)

                    def vw(v):
                        v.wait_ge(b.s_lw.h, st["wld"])
                    block.vector(vw)
                    off = 0
                    while off < nf:
                        piece = min(1024, nf - off)
                        qret = emit_quant(wld[:, off:off + piece],
                                          dst[:, off:off + piece], piece,
                                          halve=halve, dst_war=wq_war,
                                          post=post(off, piece) if post else None)
                        st["w_gp"] = qret["gp"]
                        st["w_r2"] = qret["r2"]
                        off += piece

                if first:
                    for c in range(DC):
                        def ssec(s, c=c):
                            if st["qgp"]:
                                s.wait_ge(b.s_gp.h, st["qgp"])
                            if st.get("qr2"):
                                s.wait_ge(b.s_dve.h, st["qr2"])
                            st["wld"] = b.s_lw.inc(s.dma_start(
                                out=wld[:, :HID],
                                in_=wd_d[128 * c:128 * (c + 1), :]), 16)
                        block.sync(ssec)

                        def vw(v):
                            v.wait_ge(b.s_lw.h, st["wld"])
                        block.vector(vw)
                        for off, piece in ((0, 1024), (1024, 1024), (2048, 768)):
                            def mkpost(off=off, piece=piece, c=c):
                                def post(qtok):
                                    for p2 in range(piece // 128):
                                        cc = (off + 128 * p2) // 128
                                        st["wd_ev"] = emit_tp(
                                            wq_chunk[:, 128 * p2:128 * (p2 + 1)],
                                            wdqT[:, cc, 128 * c:128 * (c + 1)],
                                            pe_wait=(b.s_dve.h, qtok))
                                    fst["wq_pe"] = b.s_pe.n
                                return post
                            qret = emit_quant(wld[:, off:off + piece],
                                              wq_chunk[:, :piece], piece,
                                              halve=True, dst_war=wq_war,
                                              post=mkpost())
                            st["w_gp"] = qret["gp"]
                            st["w_r2"] = qret["r2"]
                    flush_q()

                    def spill_wd(g, ev=st["wd_ev"]):
                        g.wait_ge(b.s_dve.h, ev)
                        if DEBUG_DUMP and "ffn" in DEBUG_SEL:
                            b.s_sp.inc(g.dma_start(out=dbg["wdq0"], in_=wdqT[:, 0, :]), 16)
                        for cc in range(HC):
                            b.s_sp.inc(g.dma_start(out=wdq_s[cc, :, :],
                                                   in_=wdqT[:, cc, :]), 16)
                    block.gpsimd(spill_wd)
                else:
                    def reload_wd(s):
                        s.wait_ge(b.s_sp.h, b.s_sp.n)
                        for cc in range(HC):
                            b.s_lwd.inc(s.dma_start(out=wdqT[:, cc, :],
                                                    in_=wdq_s[cc, :, :]), 16)
                        st["wd_re"] = b.s_lwd.n
                    block.sync(reload_wd)

                    def wait_wd(t):
                        t.wait_ge(b.s_lwd.h, st["wd_re"])
                    block.tensor(wait_wd)

                ln_slots = [wld[:, 0:D], wld[:, 1024:1024 + D]]
                for S in range(NSUP):
                    tts = [4 * S + i for i in range(4)]
                    for k, tt in enumerate(tts):
                        ln_dst = ln_slots[k & 1]
                        lntok = emit_ln(xres[:, tt, :], ln_dst)
                        if DEBUG_DUMP and "ffn" in DEBUG_SEL and first and tt == 2:
                            def dmp(g, lntok=lntok, ln_dst=ln_dst):
                                g.wait_ge(b.s_dve.h, lntok)
                                b.s_sp.inc(g.dma_start(out=dbg["ln"], in_=ln_dst), 16)
                            block.gpsimd(dmp)

                        def mkpost(k=k, tt=tt):
                            def post(qtok):
                                if (DEBUG_DUMP and "ffn" in DEBUG_SEL and first
                                        and tt == 2):
                                    def dmp2(g, qtok=qtok):
                                        g.wait_ge(b.s_dve.h, qtok)
                                        b.s_sp.inc(g.dma_start(out=dbg["xq"],
                                                               in_=xq_sb[:]), 16)
                                    block.gpsimd(dmp2)
                                ev_war = ([(b.s_pe.h, fst["sup_pe"])]
                                          if fst["sup_pe"] else [])
                                for c in range(DC):
                                    emit_tp(xq_sb[:, 128 * c:128 * (c + 1)],
                                            xqT[:, c, 128 * k:128 * k + 128],
                                            pe_wait=(b.s_dve.h, qtok),
                                            evac_war=ev_war)
                                fst["xq_pe"] = b.s_pe.n
                            return post
                        emit_quant(ln_dst, xq_sb[:], D, halve=False,
                                   dst_war=lambda: ((b.s_pe.h, fst["xq_pe"])
                                                    if fst.get("xq_pe") else None),
                                   post=mkpost())
                    flush_q()

                    for si, (soff, slen) in enumerate(STRIPS):
                        if first and S == 0:
                            for wdram, wstrip in ((wg_d, wgstrip), (wu_d, wustrip)):
                                for c2 in range(slen // 128):
                                    r0 = soff + 128 * c2

                                    def mkwpost(c2=c2, wstrip=wstrip):
                                        def factory(off, piece):
                                            def post(qtok):
                                                ev_war = []
                                                if fst["strip_pe"]:
                                                    ev_war.append(
                                                        (b.s_pe.h, fst["strip_pe"]))
                                                if fst.get("spill_tok"):
                                                    ev_war.append(
                                                        (b.s_sp.h, fst["spill_tok"]))
                                                for cc in range(DC):
                                                    st["gu_ev"] = emit_tp(
                                                        wq_chunk[:, 128 * cc:
                                                                 128 * (cc + 1)],
                                                        wstrip[:, cc, 128 * c2:
                                                               128 * c2 + 128],
                                                        pe_wait=(b.s_dve.h, qtok),
                                                        evac_war=ev_war)
                                                fst["wq_pe"] = b.s_pe.n
                                            return post
                                        return factory
                                    quant_w_chunk(wdram[r0:r0 + 128, :], D,
                                                  wq_chunk[:, :D], halve=False,
                                                  post=mkwpost())
                            flush_q()
                            last_ev = st["gu_ev"]

                            def spill_strips(g, soff=soff, slen=slen, ev=last_ev):
                                g.wait_ge(b.s_dve.h, ev)
                                for wspill, wstrip in ((wgq_s, wgstrip), (wuq_s, wustrip)):
                                    for cc in range(DC):
                                        b.s_sp.inc(g.dma_start(
                                            out=wspill[cc, :, soff:soff + slen],
                                            in_=wstrip[:, cc, :slen]), 16)
                            block.gpsimd(spill_strips)
                            fst["spill_tok"] = b.s_sp.n

                            def twait(t, ev=last_ev):
                                t.wait_ge(b.s_dve.h, ev)
                            block.tensor(twait)
                        else:
                            def reload_strips(s, soff=soff, slen=slen):
                                if fst["strip_pe"]:
                                    s.wait_ge(b.s_pe.h, fst["strip_pe"])
                                s.wait_ge(b.s_sp.h, b.s_sp.n)
                                for wspill, wstrip in ((wgq_s, wgstrip), (wuq_s, wustrip)):
                                    for cc in range(DC):
                                        b.s_lst.inc(s.dma_start(
                                            out=wstrip[:, cc, :slen],
                                            in_=wspill[cc, :, soff:soff + slen]), 16)
                                st["strip_re"] = b.s_lst.n
                            block.sync(reload_strips)

                            def twait2(t):
                                t.wait_ge(b.s_lst.h, st["strip_re"])
                            block.tensor(twait2)

                        def do_hjob(job):
                            par, k, slen2, soff2, tok = (job["par"], job["k"],
                                                         job["slen"], job["soff"],
                                                         job["tok"])

                            def asec(sc, par=par, slen2=slen2, tok=tok):
                                sc.wait_ge(b.s_pe.h, tok["g"])
                                if fst[f"silu_rd{par}"]:
                                    sc.wait_ge(b.s_dve.h, fst[f"silu_rd{par}"])
                                tok["silu"] = b.s_act.inc(sc.activation(
                                    silu_sb[par][:, :slen2], psG[par][:, :slen2],
                                    AF.Silu))
                            block.scalar(asec)
                            fst[f"sfree{par}"] = tok["silu"]

                            def vsec(v, par=par, slen2=slen2, tok=tok):
                                v.wait_ge(b.s_act.h, tok["silu"])
                                v.wait_ge(b.s_pe.h, tok["u"])
                                if fst.get(f"hgp{par}"):
                                    v.wait_ge(b.s_gp.h, fst[f"hgp{par}"])
                                tok["h"] = b.s_dve.inc(v.tensor_tensor(
                                    h_sb[par][:, :slen2], silu_sb[par][:, :slen2],
                                    psU[par][:, :slen2], ALU.mult))
                            block.vector(vsec)
                            fst[f"silu_rd{par}"] = tok["h"]
                            if job["dump"]:
                                def dmph(g, par=par, tok=tok):
                                    g.wait_ge(b.s_dve.h, tok["h"])
                                    b.s_sp.inc(g.dma_start(out=dbg["h"],
                                                           in_=h_sb[par][:, :]), 16)
                                    b.s_sp.inc(g.dma_start(out=dbg["up"],
                                                           in_=silu_sb[par][:, :]), 16)
                                block.gpsimd(dmph)

                            def mkhpost(par=par, k=k, soff2=soff2, slen2=slen2,
                                        dump=job["dump"]):
                                def post(qtok):
                                    fst[f"hpipe{par}"] = qtok
                                    if dump:
                                        def dmphq(g, par=par, qtok=qtok):
                                            g.wait_ge(b.s_dve.h, qtok)
                                            b.s_sp.inc(g.dma_start(
                                                out=dbg["hq"],
                                                in_=hq_sb[par][:, :]), 16)
                                        block.gpsimd(dmphq)
                                    ev_war = ([(b.s_pe.h, fst["sup_pe"])]
                                              if fst["sup_pe"] else [])
                                    for p2 in range(slen2 // 128):
                                        cc = (soff2 + 128 * p2) // 128
                                        emit_tp(hq_sb[par][:, 128 * p2:128 * p2 + 128],
                                                hqT[:, cc, 128 * k:128 * k + 128],
                                                pe_wait=(b.s_dve.h, qtok),
                                                evac_war=ev_war)
                                    fst[f"hq_pe{par}"] = b.s_pe.n
                                return post
                            qret = emit_quant(h_sb[par][:, :slen2],
                                              hq_sb[par][:, :slen2],
                                              slen2, halve=False,
                                              dst_war=lambda par=par: (
                                                  (b.s_pe.h, fst[f"hq_pe{par}"])
                                                  if fst.get(f"hq_pe{par}") else None),
                                              post=mkhpost())
                            fst[f"hgp{par}"] = qret["gp"]

                        for k, tt in enumerate(tts):
                            par = k % 2
                            tok = {}

                            def tsec(t, par=par, k=k, slen=slen, tok=tok):
                                if fst.get(f"sfree{par}"):
                                    t.wait_ge(b.s_act.h, fst[f"sfree{par}"])
                                if fst[f"silu_rd{par}"]:
                                    t.wait_ge(b.s_dve.h, fst[f"silu_rd{par}"])
                                ig = iu = None
                                for cc in range(DC):
                                    ig = t.matmul(psG[par][:, :slen],
                                                  xqT[:, cc, 128 * k:128 * k + 128],
                                                  wgstrip[:, cc, :slen],
                                                  start=(cc == 0), stop=(cc == DC - 1))
                                    iu = t.matmul(psU[par][:, :slen],
                                                  xqT[:, cc, 128 * k:128 * k + 128],
                                                  wustrip[:, cc, :slen],
                                                  start=(cc == 0), stop=(cc == DC - 1))
                                tok["g"] = b.s_pe.inc(ig)
                                tok["u"] = b.s_pe.inc(iu)
                            block.tensor(tsec)
                            fst["strip_pe"] = b.s_pe.n
                            if fst.get("hjob") is not None:
                                do_hjob(fst["hjob"])
                            fst["hjob"] = dict(
                                par=par, k=k, slen=slen, soff=soff, tok=tok,
                                dump=(DEBUG_DUMP and "ffn" in DEBUG_SEL and first
                                      and S == 0 and si == 0 and k == 2))

                    if fst.get("hjob") is not None:
                        do_hjob(fst["hjob"])
                        fst["hjob"] = None
                    flush_q()
                    for k, tt in enumerate(tts):
                        tok = {}

                        def tsec(t, k=k, tok=tok):
                            for f in range(2):
                                if fst[f"dres{f}"]:
                                    t.wait_ge(b.s_dve.h, fst[f"dres{f}"])
                                im = None
                                for cc in range(HC):
                                    im = t.matmul(psD[f][:, :],
                                                  hqT[:, cc, 128 * k:128 * k + 128],
                                                  wdqT[:, cc, 512 * f:512 * f + 512],
                                                  start=(cc == 0), stop=(cc == HC - 1))
                                tok[f"d{f}"] = b.s_pe.inc(im)
                        block.tensor(tsec)

                        def vsec(v, tt=tt, tok=tok):
                            for f in range(2):
                                v.wait_ge(b.s_pe.h, tok[f"d{f}"])
                                fst[f"dres{f}"] = b.s_dve.inc(v.tensor_add(
                                    xres[:, tt, 512 * f:512 * f + 512],
                                    xres[:, tt, 512 * f:512 * f + 512], psD[f][:, :]))
                        block.vector(vsec)
                    fst["sup_pe"] = b.s_pe.n

            barrier()

        def attention():
            actx = ExitStack()
            with actx:
                aent = actx.enter_context
                qT = aent(nc.sbuf_tensor("qT", [128, DC, T], BF16))
                kT = aent(nc.sbuf_tensor("kT", [128, DC, T], BF16))
                v_sb = aent(nc.sbuf_tensor("v_sb", [128, 16, 1040], BF16))
                ast = {"tp": 0, "evac0": 0, "evac1": 0}

                pctx = ExitStack()
                with pctx:
                    pent = pctx.enter_context
                    ln2T = pent(nc.sbuf_tensor("ln2T", [128, DC, T], BF16))
                    ln2c = pent(nc.sbuf_tensor("ln2c", [128, D], BF16))
                    wproj = pent(nc.sbuf_tensor("wproj", [128, DC, 128], BF16))
                    vtmp = pent(nc.sbuf_tensor("vtmp", [128, 512], BF16))
                    psP = [pent(nc.psum_tensor(f"psP{i}", [128, 512], F32)) for i in range(2)]
                    psTT = [pent(nc.psum_tensor(f"psTT{i}", [128, 128], BF16)) for i in range(2)]

                    def emit_tp2(src, dst, pe_wait=None, evac_war=None):
                        i = ast["tp"] % 2
                        ast["tp"] += 1
                        tk = {}

                        def tsec(t):
                            if pe_wait is not None:
                                t.wait_ge(pe_wait[0], pe_wait[1])
                            if ast[f"evac{i}"]:
                                t.wait_ge(b.s_dve.h, ast[f"evac{i}"])
                            tk["tp"] = b.s_pe.inc(t.transpose(psTT[i][:], src, idn[:]))
                        block.tensor(tsec)

                        def vsec(v):
                            v.wait_ge(b.s_pe.h, tk["tp"])
                            if evac_war is not None:
                                v.wait_ge(evac_war[0], evac_war[1])
                            tk["ev"] = b.s_dve.inc(v.tensor_copy(dst, psTT[i][:]))
                        block.vector(vsec)
                        ast[f"evac{i}"] = tk["ev"]
                        return tk["ev"]

                    lamtok = {}

                    def lam_v(v):
                        t = b.s_dve.inc(v.tensor_mul(rowscr[0:1, 128:160],
                                                     rowscr[0:1, 0:32],
                                                     rowscr[0:1, 32:64]))
                        v.wait_ge(b.s_dve.h, t)
                        t = b.s_dve.inc(v.tensor_mul(rowscr[0:1, 160:192],
                                                     rowscr[0:1, 64:96],
                                                     rowscr[0:1, 96:128]))
                        v.wait_ge(b.s_dve.h, t)
                        t = b.s_dve.inc(v.reduce_sum(rowscr[0:1, 192:193],
                                                     rowscr[0:1, 128:160], AX.X))
                        v.wait_ge(b.s_dve.h, t)
                        t = b.s_dve.inc(v.reduce_sum(rowscr[0:1, 193:194],
                                                     rowscr[0:1, 160:192], AX.X))
                        v.wait_ge(b.s_dve.h, t)
                        lamtok["sums"] = t
                    block.vector(lam_v)

                    def lam_a(sc):
                        sc.wait_ge(b.s_dve.h, lamtok["sums"])
                        lamtok["exps"] = b.s_act.inc(sc.activation(
                            rowscr[0:1, 194:196], rowscr[0:1, 192:194], AF.Exp))
                    block.scalar(lam_a)

                    def lam_v2(v):
                        v.wait_ge(b.s_act.h, lamtok["exps"])
                        t = b.s_dve.inc(v.tensor_sub(rowscr[0:1, 196:197],
                                                     rowscr[0:1, 194:195],
                                                     rowscr[0:1, 195:196]))
                        v.wait_ge(b.s_dve.h, t)
                        lamtok["lam"] = b.s_dve.inc(v.tensor_single_scalar(
                            rowscr[0:1, 196:197], rowscr[0:1, 196:197], LAM0, ALU.add))
                        v.wait_ge(b.s_dve.h, lamtok["lam"])
                    block.vector(lam_v2)

                    def lam_t(t):
                        t.wait_ge(b.s_dve.h, lamtok["lam"])
                        lamtok["bc"] = b.s_pe.inc(t.matmul(
                            psP[0][:, 0:1], ones[0:1, :], rowscr[0:1, 196:197],
                            start=True, stop=True))
                    block.tensor(lam_t)

                    def lam_v3(v):
                        v.wait_ge(b.s_pe.h, lamtok["bc"])
                        t = b.s_dve.inc(v.tensor_copy(lamcol[:], psP[0][:, 0:1]))
                        v.wait_ge(b.s_dve.h, t)
                    block.vector(lam_v3)

                    for tt in range(TC if ATTN_STAGE >= 2 else 0):
                        war = (b.s_pe.h, ast.get("ln2c_pe", 0)) if ast.get("ln2c_pe") else None
                        lntok2 = emit_ln(xres[:, tt, :], ln2c[:], dst_war=war)
                        for c in range(DC):
                            emit_tp2(ln2c[:, 128 * c:128 * (c + 1)],
                                     ln2T[:, c, 128 * tt:128 * tt + 128],
                                     pe_wait=(b.s_dve.h, lntok2))
                        ast["ln2c_pe"] = b.s_pe.n
                    ast["ln2T_ev"] = b.s_dve.n

                    def project_fm(w_dram, dstT):
                        for fo in range(DC):
                            ctok = {}

                            def gsec(g, fo=fo, ctok=ctok):
                                if ast.get("wproj_pe"):
                                    g.wait_ge(b.s_pe.h, ast["wproj_pe"])
                                for e in range(DC):
                                    ctok[e] = b.s_cast.inc(g.dma_start(
                                        out=wproj[:, e, :],
                                        in_=w_dram[128 * e:128 * (e + 1),
                                                   128 * fo:128 * (fo + 1)]), 16)
                            block.gpsimd(gsec)

                            for ts4 in range(4):
                                par = ts4 % 2
                                mtok = {}

                                def tsec(t, ts4=ts4, par=par, ctok=ctok, mtok=mtok):
                                    t.wait_ge(b.s_cast.h, ctok[DC - 1])
                                    t.wait_ge(b.s_dve.h, ast["ln2T_ev"])
                                    if ast.get(f"pevac{par}"):
                                        t.wait_ge(b.s_act.h, ast[f"pevac{par}"])
                                    im = None
                                    for e in range(DC):
                                        im = t.matmul(psP[par][:], wproj[:, e, :],
                                                      ln2T[:, e, 512 * ts4:512 * ts4 + 512],
                                                      start=(e == 0), stop=(e == DC - 1))
                                    mtok["m"] = b.s_pe.inc(im)
                                block.tensor(tsec)
                                ast["wproj_pe"] = b.s_pe.n

                                def asecE(sc, fo=fo, ts4=ts4, par=par, mtok=mtok, dstT=dstT):
                                    sc.wait_ge(b.s_pe.h, mtok["m"])
                                    ast[f"pevac{par}"] = b.s_act.inc(sc.copy(
                                        dstT[:, fo, 512 * ts4:512 * ts4 + 512],
                                        psP[par][:]))
                                block.scalar(asecE)

                    if ATTN_STAGE >= 3:
                        project_fm(wq_d, qT)
                        project_fm(wk_d, kT)

                    def vv(v):
                        t = b.s_dve.inc(v.memset(
                            v_sb[:].rearrange("p s (hh c) -> p s hh c", c=65)[:, :, :, 64:65],
                            1.0))
                        v.wait_ge(b.s_dve.h, t)
                    block.vector(vv)

                    for dc_i in range(DC if ATTN_STAGE >= 4 else 0):
                        ctok = {}

                        def gsec(g, dc_i=dc_i, ctok=ctok):
                            if ast.get("wproj_pe"):
                                g.wait_ge(b.s_pe.h, ast["wproj_pe"])
                            for e in range(DC):
                                ctok[e] = b.s_cast.inc(g.dma_start(
                                    out=wproj[:, e, :],
                                    in_=wv_d[128 * e:128 * (e + 1),
                                             128 * dc_i:128 * (dc_i + 1)]), 16)
                        block.gpsimd(gsec)

                        for ts4 in range(4):
                            par = ts4 % 2
                            mtok = {}

                            def tsec(t, ts4=ts4, par=par, ctok=ctok, mtok=mtok):
                                t.wait_ge(b.s_cast.h, ctok[DC - 1])
                                t.wait_ge(b.s_dve.h, ast["ln2T_ev"])
                                if ast.get(f"pevac{par}"):
                                    t.wait_ge(b.s_act.h, ast[f"pevac{par}"])
                                im = None
                                for e in range(DC):
                                    im = t.matmul(psP[par][:], wproj[:, e, :],
                                                  ln2T[:, e, 512 * ts4:512 * ts4 + 512],
                                                  start=(e == 0), stop=(e == DC - 1))
                                mtok["m"] = b.s_pe.inc(im)
                            block.tensor(tsec)
                            ast["wproj_pe"] = b.s_pe.n

                            def asecE(sc, par=par, mtok=mtok):
                                sc.wait_ge(b.s_pe.h, mtok["m"])
                                if ast.get("vtmp_pe"):
                                    sc.wait_ge(b.s_pe.h, ast["vtmp_pe"])
                                mtok["ev"] = b.s_act.inc(sc.copy(vtmp[:], psP[par][:]))
                            block.scalar(asecE)
                            ast[f"pevac{par}"] = mtok["ev"]

                            for p2 in range(4):
                                sc_i = 4 * ts4 + p2
                                dstv = v_sb[:, sc_i, :].rearrange(
                                    "p (hh c) -> p hh c", c=65)[:, 2 * dc_i:2 * dc_i + 2, 0:64]
                                emit_tp2(vtmp[:, 128 * p2:128 * p2 + 128], dstv,
                                         pe_wait=(b.s_act.h, mtok["ev"]))
                            ast["vtmp_pe"] = b.s_pe.n

                    if DEBUG_DUMP and "proj" in DEBUG_SEL:
                        def dmpp(g_):
                            g_.wait_ge(b.s_dve.h, b.s_dve.n)
                            b.s_sp.inc(g_.dma_start(out=dbg["qt0"], in_=qT[:, 0, 0:512]), 16)
                            b.s_sp.inc(g_.dma_start(out=dbg["ln2t0"], in_=ln2T[:, 0, 0:512]), 16)
                            b.s_sp.inc(g_.dma_start(out=dbg["kt0"], in_=kT[:, 0, :]), 16)
                            b.s_sp.inc(g_.dma_start(out=dbg["v0"], in_=v_sb[:, 0, :]), 16)
                        block.gpsimd(dmpp)
                    barrier()

                sctx = ExitStack()
                with sctx:
                    sent = sctx.enter_context
                    e_sb = sent(nc.sbuf_tensor("e_sb", [128, 4, 2, 512], BF16))
                    nrm = sent(nc.sbuf_tensor("nrm", [128, 16, 512], BF16))
                    scr_o = sent(nc.sbuf_tensor("scr_o", [128, 512], F32))
                    scr_t = sent(nc.sbuf_tensor("scr_t", [128, 512], F32))
                    scr_cr = sent(nc.sbuf_tensor("scr_cr", [128, 512], F32))
                    wob = sent(nc.sbuf_tensor("wob", [128, 4, 512], BF16))
                    wob2 = sent(nc.sbuf_tensor("wob2", [128, 4, 512], BF16))
                    psSc = [sent(nc.psum_tensor(f"psSc{i}", [128, 512], F32)) for i in range(4)]
                    psA = [sent(nc.psum_tensor(f"psA{i}", [128, 512], F32)) for i in range(4)]

                    sst = {}
                    for i in range(4):
                        sst[f"exp{i}"] = 0
                        sst[f"av{i}0"] = 0
                        sst[f"av{i}1"] = 0
                        sst[f"scfree{i}"] = None
                        sst[f"pafree{i}"] = None
                    sst["wo_pe"] = 0
                    sst["nrm_dve"] = 0

                    def guard_v(v, inst):
                        t = b.s_dve.inc(inst)
                        v.wait_ge(b.s_dve.h, t)
                        return t

                    def guard_a(sc, inst):
                        t = b.s_act.inc(inst)
                        sc.wait_ge(b.s_act.h, t)
                        return t

                    for tau in range(ATTN_NTAU if ATTN_STAGE >= 5 else 0):
                        for g in range(ATTN_NG):
                            avtok = {}
                            for s_i in range(ATTN_NS):
                                par = s_i % 2
                                qktok = {}

                                def tq(t, g=g, tau=tau, s_i=s_i, qktok=qktok):
                                    if ATTN_SUB == 5:
                                        for i in range(4):
                                            if sst[f"scfree{i}"] is not None:
                                                w = sst[f"scfree{i}"]
                                                t.wait_ge(w[0], w[1])
                                                sst[f"scfree{i}"] = None
                                        for i in range(4):
                                            if i == 2:
                                                t.wait_ge(b.s_pe.h, qktok[1])
                                            t.ldweights(
                                                kT[32*i:32*i+32, g, 128*s_i:128*s_i+128],
                                                tile_position=(32*i, 0))
                                            qktok[i] = b.s_pe.inc(t.matmul(
                                                psSc[i][:],
                                                kT[32*i:32*i+32, g, 128*s_i:128*s_i+128],
                                                qT[32*i:32*i+32, g, 512*tau:512*tau+512],
                                                start=True, stop=True,
                                                tile_position=(32*i, 0)))
                                        return
                                    if ATTN_SUB == -1:
                                        for i in range(3):
                                            t.ldweights(kT[32*i:32*i+32, g, 128*s_i:128*s_i+128],
                                                        tile_position=(32*i, 0))
                                            qktok[i] = b.s_pe.inc(t.matmul(
                                                psSc[i][:],
                                                kT[32*i:32*i+32, g, 128*s_i:128*s_i+128],
                                                qT[32*i:32*i+32, g, 512*tau:512*tau+512],
                                                start=True, stop=True,
                                                tile_position=(32*i, 0)))
                                        qktok[3] = qktok[2]
                                        return
                                    if ATTN_SUB == 0:
                                        qktok[0] = b.s_pe.inc(t.matmul(
                                            psSc[0][:], kT[:, 0, 0:128], kT[:, 0, 0:512],
                                            start=True, stop=True))
                                        for i in range(1, 4):
                                            qktok[i] = qktok[0]
                                        return
                                    if ATTN_SUB == -2:
                                        qktok[0] = b.s_pe.inc(t.matmul(
                                            psSc[0][:], kT[0:32, 0, 0:128], kT[0:32, 0, 0:512],
                                            start=True, stop=True))
                                        for i in range(1, 4):
                                            qktok[i] = qktok[0]
                                        return
                                    if ATTN_SUB == -4:
                                        for i in range(2):
                                            t.ldweights(kT[32*i:32*i+32, 0, 0:128],
                                                        tile_position=(32*i, 0))
                                            qktok[i] = b.s_pe.inc(t.matmul(
                                                psSc[i][:], kT[32*i:32*i+32, 0, 0:128],
                                                kT[32*i:32*i+32, 0, 0:512],
                                                start=True, stop=True,
                                                tile_position=(32*i, 0)))
                                        qktok[2] = qktok[1]; qktok[3] = qktok[1]
                                        return
                                    if ATTN_SUB == -3:
                                        t.ldweights(kT[32:64, 0, 0:128], tile_position=(32, 0))
                                        qktok[0] = b.s_pe.inc(t.matmul(
                                            psSc[0][:], kT[32:64, 0, 0:128], kT[32:64, 0, 0:512],
                                            start=True, stop=True, tile_position=(32, 0)))
                                        for i in range(1, 4):
                                            qktok[i] = qktok[0]
                                        return
                                    for i in range(4):
                                        if sst[f"scfree{i}"] is not None:
                                            w = sst[f"scfree{i}"]
                                            t.wait_ge(w[0], w[1])
                                            sst[f"scfree{i}"] = None
                                        if i == 2:
                                            t.wait_ge(b.s_pe.h, qktok[1])
                                        t.ldweights(
                                            kT[32 * i:32 * i + 32, g, 128 * s_i:128 * s_i + 128],
                                            tile_position=(32 * i, 0))
                                        qktok[i] = b.s_pe.inc(t.matmul(
                                            psSc[i][:],
                                            kT[32 * i:32 * i + 32, g, 128 * s_i:128 * s_i + 128],
                                            qT[32 * i:32 * i + 32, g, 512 * tau:512 * tau + 512],
                                            start=True, stop=True,
                                            tile_position=(32 * i, 0)))
                                block.tensor(tq)
                                if ATTN_SUB < 2:
                                    continue

                                def ta(sc, g=g, s_i=s_i, par=par, qktok=qktok):
                                    for i in range(4):
                                        sc.wait_ge(b.s_pe.h, qktok[i])
                                        if sst[f"av{i}{par}"]:
                                            sc.wait_ge(b.s_pe.h, sst[f"av{i}{par}"])
                                        sst[f"exp{i}"] = b.s_act.inc(sc.activation(
                                            e_sb[:, i, par, :], psSc[i][:], AF.Exp,
                                            scale=SCAL))
                                block.scalar(ta)
                                if ATTN_SUB < 3:
                                    continue

                                def tv(t, g=g, s_i=s_i, par=par, avtok=avtok):
                                    if ATTN_STAGE == 45:
                                        for i in range(4):
                                            t.wait_ge(b.s_act.h, sst[f"exp{i}"])
                                            sst[f"av{i}{par}"] = b.s_pe.inc(t.nop())
                                            avtok[i] = sst[f"av{i}{par}"]
                                        return
                                    for i in range(4):
                                        t.wait_ge(b.s_act.h, sst[f"exp{i}"])
                                        if s_i == 0 and sst[f"pafree{i}"] is not None:
                                            w = sst[f"pafree{i}"]
                                            t.wait_ge(w[0], w[1])
                                            sst[f"pafree{i}"] = None
                                        vj = 2 * g + i // 2
                                        im = t.matmul(
                                            psA[i][0:65, :],
                                            v_sb[:, s_i, 65 * vj:65 * vj + 65],
                                            e_sb[:, i, par, :],
                                            start=(s_i == 0), stop=(s_i == ATTN_NS - 1))
                                        sst[f"av{i}{par}"] = b.s_pe.inc(im)
                                        avtok[i] = sst[f"av{i}{par}"]
                                block.tensor(tv)

                            for jj in range(2 if ATTN_STAGE >= 6 else 0):
                                A0 = psA[2 * jj]
                                A1 = psA[2 * jj + 1]
                                jg = 2 * g + jj
                                ep = {}
                                rs0 = rowscr[64:65, 0:512]
                                rs1 = rowscr[64:65, 512:1024]
                                rb0 = rowscr[0:1, 0:512]
                                rb1 = rowscr[0:1, 512:1024]

                                def ev1(v, A0=A0, A1=A1, ep=ep):
                                    v.wait_ge(b.s_pe.h, ep["avmax"])
                                    if ep.get("rowscr_pe"):
                                        v.wait_ge(b.s_pe.h, ep["rowscr_pe"])
                                    guard_v(v, v.tensor_copy(rs0, A0[64:65, :]))
                                    guard_v(v, v.tensor_copy(rs1, A1[64:65, :]))
                                    guard_v(v, v.reciprocal(rs1, rs1))
                                    guard_v(v, v.tensor_mul(rs1, rs0, rs1))
                                    ep["crow"] = guard_v(v, v.tensor_scalar(
                                        rs1, rs1, lamcol[64:65, 0:1], None, ALU.mult))
                                ep["avmax"] = max(avtok.values())
                                ep["rowscr_pe"] = sst.get("row_pe", 0)
                                block.vector(ev1)

                                def et1(t, ep=ep, jj=jj):
                                    t.wait_ge(b.s_dve.h, ep["crow"])
                                    ep["mmC"] = b.s_pe.inc(t.matmul(
                                        psSc[0][0:64, :], ones[64:65, 0:64], rs1,
                                        start=True, stop=True, tile_position=(64, 0)))
                                    ep["mmE"] = b.s_pe.inc(t.matmul(
                                        psSc[1][0:1, :], ones[64:65, 0:1], rs0,
                                        start=True, stop=True, tile_position=(64, 0)))
                                block.tensor(et1)
                                sst["row_pe"] = ep["mmE"]

                                def ev2(v, A0=A0, A1=A1, ep=ep):
                                    v.wait_ge(b.s_pe.h, ep["mmC"])
                                    ep["c6"] = guard_v(v, v.tensor_copy(scr_cr[0:64, :],
                                                                        psSc[0][0:64, :]))
                                    v.tensor_mul(scr_t[0:64, :], A1[0:64, :], scr_cr[0:64, :])
                                    ep["o"] = guard_v(v, v.tensor_sub(
                                        scr_o[0:64, :], A0[0:64, :], scr_t[0:64, :]))
                                    ep["sq"] = guard_v(v, v.tensor_mul(
                                        scr_t[0:64, :], scr_o[0:64, :], scr_o[0:64, :]))
                                    v.wait_ge(b.s_pe.h, ep["mmE"])
                                    ep["er"] = guard_v(v, v.tensor_copy(rb0, psSc[1][0:1, :]))
                                    guard_v(v, v.tensor_mul(rb1, rb0, rb0))
                                    ep["ersq"] = guard_v(v, v.tensor_single_scalar(
                                        rb1, rb1, 1e-5, ALU.mult))
                                block.vector(ev2)

                                def et2(t, ep=ep):
                                    t.wait_ge(b.s_dve.h, ep["sq"])
                                    ep["mmS"] = b.s_pe.inc(t.matmul(
                                        psSc[2][0:1, :], ones[0:64, 0:1], scr_t[0:64, :],
                                        start=True, stop=True))
                                block.tensor(et2)

                                def ev3(v, ep=ep):
                                    v.wait_ge(b.s_pe.h, ep["mmS"])
                                    ep["u2"] = guard_v(v, v.scalar_tensor_tensor(
                                        rb0, psSc[2][0:1, :], 1.0 / 64.0, rb1,
                                        ALU.mult, ALU.add))
                                block.vector(ev3)

                                def ea1(sc, ep=ep):
                                    sc.wait_ge(b.s_dve.h, ep["u2"])
                                    guard_a(sc, sc.activation(rb1, rb0, AF.Ln))
                                    ep["rstd"] = guard_a(sc, sc.activation(
                                        rb1, rb1, AF.Exp, scale=-0.5))
                                block.scalar(ea1)

                                def et3(t, ep=ep):
                                    t.wait_ge(b.s_act.h, ep["rstd"])
                                    ep["mmR"] = b.s_pe.inc(t.matmul(
                                        psSc[3][0:64, :], ones[0:1, 0:64], rb1,
                                        start=True, stop=True))
                                block.tensor(et3)
                                sst["row_pe"] = ep["mmR"]

                                def ev4(v, ep=ep, jg=jg):
                                    v.wait_ge(b.s_pe.h, ep["mmR"])
                                    ep["c14"] = guard_v(v, v.tensor_copy(
                                        scr_cr[0:64, :], psSc[3][0:64, :]))
                                    v.tensor_mul(scr_t[0:64, :], scr_o[0:64, :],
                                                 scr_cr[0:64, :])
                                    if sst["wo_pe"]:
                                        v.wait_ge(b.s_pe.h, sst["wo_pe"])
                                    ep["nrm"] = b.s_dve.inc(v.tensor_scalar(
                                        nrm[0:64, jg, :], scr_t[0:64, :],
                                        sublncol[0:64, 0:1], None, ALU.mult))
                                block.vector(ev4)
                                sst["nrm_dve"] = ep["nrm"]
                                if DEBUG_DUMP and "epi" in DEBUG_SEL and tau == 0 and g == 0 and jj == 0:
                                    def dmp(g_, ep=ep):
                                        g_.wait_ge(b.s_dve.h, ep["nrm"])
                                        b.s_sp.inc(g_.dma_start(out=dbg["lamcol"], in_=lamcol[:]), 16)
                                        b.s_sp.inc(g_.dma_start(out=dbg["rowscr"], in_=rowscr[:]), 16)
                                        b.s_sp.inc(g_.dma_start(out=dbg["scro"], in_=scr_o[:]), 16)
                                        b.s_sp.inc(g_.dma_start(out=dbg["nrm0"], in_=nrm[:, 0, :]), 16)
                                        b.s_sp.inc(g_.dma_start(out=dbg["e0"], in_=e_sb[:, 0, 1, :]), 16)
                                    block.gpsimd(dmp)

                                sst[f"pafree{2 * jj}"] = (b.s_dve.h, ep["o"])
                                sst[f"pafree{2 * jj + 1}"] = (b.s_dve.h, ep["o"])
                                sst["scfree0"] = (b.s_dve.h, ep["c6"])
                                sst["scfree1"] = (b.s_dve.h, ep["er"])
                                sst["scfree2"] = (b.s_dve.h, ep["u2"])
                                sst["scfree3"] = (b.s_dve.h, ep["c14"])

                        if DEBUG_DUMP and "nrm" in DEBUG_SEL and tau == 0:
                            def dmpn(g_):
                                g_.wait_ge(b.s_dve.h, sst["nrm_dve"])
                                b.s_sp.inc(g_.dma_start(out=dbg["nrmall"], in_=nrm[:]), 16)
                            block.gpsimd(dmpn)
                        for fo in range(2 if ATTN_STAGE >= 7 else 0):
                            for cc in range(8):
                                par = cc % 4
                                wtok = {}

                                castsem = b.s_cast if par % 2 == 0 else b.s_cast1

                                def gsec(g_, fo=fo, cc=cc, par=par, wtok=wtok,
                                         castsem=castsem):
                                    if sst.get(f"wob_pe{par}"):
                                        g_.wait_ge(b.s_pe.h, sst[f"wob_pe{par}"])
                                    castsem.inc(g_.dma_start(
                                        out=wob[0:64, par, :],
                                        in_=wo_d[128 * cc:128 * cc + 64,
                                                 512 * fo:512 * fo + 512]), 16)
                                    wtok["ld"] = castsem.inc(g_.dma_start(
                                        out=wob2[0:64, par, :],
                                        in_=wo_d[128 * cc + 64:128 * cc + 128,
                                                 512 * fo:512 * fo + 512]), 16)
                                block.gpsimd(gsec)

                                def tsec(t, fo=fo, cc=cc, par=par, wtok=wtok, tau=tau,
                                         castsem=castsem):
                                    t.wait_ge(castsem.h, wtok["ld"])
                                    if cc == 0:
                                        t.wait_ge(b.s_dve.h, sst["nrm_dve"])
                                        for tt in range(4):
                                            w = sst[f"pafree{tt}"]
                                            if w is not None:
                                                t.wait_ge(w[0], w[1])
                                                sst[f"pafree{tt}"] = None
                                            if sst.get(f"wores{tt}"):
                                                t.wait_ge(b.s_dve.h, sst[f"wores{tt}"])
                                    im = None
                                    for tt in range(4):
                                        im = t.matmul(
                                            psA[tt][:],
                                            nrm[0:64, 2 * cc, 128 * tt:128 * tt + 128],
                                            wob[0:64, par, :],
                                            start=(cc == 0), stop=False)
                                        last = (cc == 7)
                                        im = t.matmul(
                                            psA[tt][:],
                                            nrm[0:64, 2 * cc + 1, 128 * tt:128 * tt + 128],
                                            wob2[0:64, par, :],
                                            start=False, stop=last)
                                        wtok[f"m{tt}"] = b.s_pe.inc(im)
                                    sst[f"wob_pe{par}"] = wtok["m3"]
                                block.tensor(tsec)

                                if cc == 7:
                                    def vres(v, fo=fo, tau=tau, wtok=wtok):
                                        for tt in range(4):
                                            v.wait_ge(b.s_pe.h, wtok[f"m{tt}"])
                                            gt = 4 * tau + tt
                                            sst[f"wores{tt}"] = b.s_dve.inc(v.tensor_add(
                                                xres[:, gt, 512 * fo:512 * fo + 512],
                                                xres[:, gt, 512 * fo:512 * fo + 512],
                                                psA[tt][:]))
                                    block.vector(vres)
                        sst["wo_pe"] = b.s_pe.n
                        for tt in range(4):
                            if sst.get(f"wores{tt}"):
                                sst[f"pafree{tt}"] = (b.s_dve.h, sst[f"wores{tt}"])

                    barrier()

        ffn(first=True)
        if n_phases >= 2:
            attention()
        if n_phases >= 3:
            ffn(first=False)

        barrier()

        def store_out(s):
            for t in range(TC):
                b.s_st.inc(s.dma_start(out=out_d[128 * t:128 * (t + 1), :],
                                       in_=xres[:, t, :]), 16)
            s.wait_ge(b.s_st.h, b.s_st.n)
        block.sync(store_out)

    return nc


def kernel(L0_attn_query, L0_attn_Wq, L0_attn_Wk, L0_attn_Wv, L0_attn_Wo,
           L0_attn_lambda_q1, L0_attn_lambda_k1, L0_attn_lambda_q2,
           L0_attn_lambda_k2, L0_attn_subln_w, L0_ffn_w_gate, L0_ffn_w_up,
           L0_ffn_w_down):
    x = np.ascontiguousarray(np.asarray(L0_attn_query, dtype=np.float32))
    nc = build_program(N_PHASES)
    lam4 = np.concatenate([np.asarray(L0_attn_lambda_q1, np.float32).ravel(),
                           np.asarray(L0_attn_lambda_k1, np.float32).ravel(),
                           np.asarray(L0_attn_lambda_q2, np.float32).ravel(),
                           np.asarray(L0_attn_lambda_k2, np.float32).ravel()])
    common = {
        "wq": np.ascontiguousarray(np.asarray(L0_attn_Wq, np.float32).T),
        "wk": np.ascontiguousarray(np.asarray(L0_attn_Wk, np.float32).T),
        "wv": np.ascontiguousarray(np.asarray(L0_attn_Wv, np.float32).T),
        "wo": np.ascontiguousarray(np.asarray(L0_attn_Wo, np.float32).T),
        "wg": np.ascontiguousarray(np.asarray(L0_ffn_w_gate, np.float32)),
        "wu": np.ascontiguousarray(np.asarray(L0_ffn_w_up, np.float32)),
        "wd": np.ascontiguousarray(np.asarray(L0_ffn_w_down, np.float32)),
        "lam4": np.ascontiguousarray(lam4.reshape(1, 128)),
        "subln": np.asarray(L0_attn_subln_w, np.float32).reshape(64, 1).copy(),
        "idn": np.eye(128, dtype=BF),
        "onesrow": np.ones((128, 128), np.float32),
    }
    in_maps = [dict(common, x=np.ascontiguousarray(x[c])) for c in range(8)]
    res = run_bass_kernel_spmd(nc, in_maps, core_ids=list(range(8)))
    out = np.stack([np.asarray(res.results[c]["out"]) for c in range(8)], axis=0)
    return out.astype(np.float32)


# version 09
# speedup vs baseline: 1.0294x; 1.0294x over previous
import math
import numpy as np
import ml_dtypes
from contextlib import ExitStack

import concourse.bass as bass
import concourse.mybir as mybir
from concourse.bass_utils import run_bass_kernel_spmd

F32 = mybir.dt.float32
BF16 = mybir.dt.bfloat16
I32 = mybir.dt.int32
I16 = mybir.dt.int16
AF = mybir.ActivationFunctionType
ALU = mybir.AluOpType
AX = mybir.AxisListType
BF = ml_dtypes.bfloat16

T = 2048
TC = 16
D = 1024
DC = 8
HID = 2816
HC = 22
STRIPS = [(0, 512), (512, 512), (1024, 512), (1536, 512), (2048, 512), (2560, 256)]
NSUP = 4
LAM0 = float(0.8 - 0.6 * math.exp(-0.3 * 12))
SCAL = 1.0 / math.sqrt(32.0)

N_PHASES = 3
DEBUG_DUMP = False
DEBUG_SEL = "ffn,proj,epi,nrm"
ATTN_STAGE = 9
ATTN_NTAU = 4
ATTN_NG = 8
ATTN_NS = 16
ATTN_SUB = 3


class Sem:
    def __init__(self, ctx, nc, name):
        self.h = ctx.enter_context(nc.semaphore(name))
        self.n = 0

    def inc(self, inst, k=1):
        inst.then_inc(self.h, k)
        self.n += k
        return self.n


def build_program(n_phases=3):
    nc = bass.Bass("TRN2", target_bir_lowering=False)

    x_d = nc.dram_tensor("x", [T, D], F32, kind="ExternalInput").ap()
    wq_d = nc.dram_tensor("wq", [D, D], F32, kind="ExternalInput").ap()
    wk_d = nc.dram_tensor("wk", [D, D], F32, kind="ExternalInput").ap()
    wv_d = nc.dram_tensor("wv", [D, D], F32, kind="ExternalInput").ap()
    wo_d = nc.dram_tensor("wo", [D, D], F32, kind="ExternalInput").ap()
    wg_d = nc.dram_tensor("wg", [HID, D], F32, kind="ExternalInput").ap()
    wu_d = nc.dram_tensor("wu", [HID, D], F32, kind="ExternalInput").ap()
    wd_d = nc.dram_tensor("wd", [D, HID], F32, kind="ExternalInput").ap()
    lam_d = nc.dram_tensor("lam4", [1, 128], F32, kind="ExternalInput").ap()
    subln_d = nc.dram_tensor("subln", [64, 1], F32, kind="ExternalInput").ap()
    idn_d = nc.dram_tensor("idn", [128, 128], BF16, kind="ExternalInput").ap()
    ones_d = nc.dram_tensor("onesrow", [128, 128], F32, kind="ExternalInput").ap()
    out_d = nc.dram_tensor("out", [T, D], F32, kind="ExternalOutput").ap()
    dbg = {}
    if DEBUG_DUMP:
        dbg["ln"] = nc.dram_tensor("dbg_ln", [128, D], F32, kind="ExternalOutput").ap()
        dbg["xq"] = nc.dram_tensor("dbg_xq", [128, D], BF16, kind="ExternalOutput").ap()
        dbg["gate"] = nc.dram_tensor("dbg_gate", [128, 512], F32, kind="ExternalOutput").ap()
        dbg["up"] = nc.dram_tensor("dbg_up", [128, 512], F32, kind="ExternalOutput").ap()
        dbg["h"] = nc.dram_tensor("dbg_h", [128, 512], F32, kind="ExternalOutput").ap()
        dbg["hq"] = nc.dram_tensor("dbg_hq", [128, 512], BF16, kind="ExternalOutput").ap()
        dbg["wdq0"] = nc.dram_tensor("dbg_wdq0", [128, D], BF16, kind="ExternalOutput").ap()
        dbg["lamcol"] = nc.dram_tensor("dbg_lamcol", [128, 1], F32, kind="ExternalOutput").ap()
        dbg["rowscr"] = nc.dram_tensor("dbg_rowscr", [128, 1024], F32, kind="ExternalOutput").ap()
        dbg["scro"] = nc.dram_tensor("dbg_scro", [128, 512], F32, kind="ExternalOutput").ap()
        dbg["nrm0"] = nc.dram_tensor("dbg_nrm0", [128, 512], BF16, kind="ExternalOutput").ap()
        dbg["e0"] = nc.dram_tensor("dbg_e0", [128, 512], BF16, kind="ExternalOutput").ap()
        dbg["qt0"] = nc.dram_tensor("dbg_qt0", [128, 512], BF16, kind="ExternalOutput").ap()
        dbg["kt0"] = nc.dram_tensor("dbg_kt0", [128, 2048], BF16, kind="ExternalOutput").ap()
        dbg["v0"] = nc.dram_tensor("dbg_v0", [128, 1040], BF16, kind="ExternalOutput").ap()
        dbg["nrmall"] = nc.dram_tensor("dbg_nrmall", [128, 16, 512], BF16, kind="ExternalOutput").ap()
        dbg["ln2t0"] = nc.dram_tensor("dbg_ln2t0", [128, 512], BF16, kind="ExternalOutput").ap()
    wgq_s = nc.dram_tensor("wgq_s", [DC, 128, HID], BF16, kind="Internal").ap()
    wuq_s = nc.dram_tensor("wuq_s", [DC, 128, HID], BF16, kind="Internal").ap()
    wdq_s = nc.dram_tensor("wdq_s", [HC, 128, D], BF16, kind="Internal").ap()

    ctx = ExitStack()
    with ctx:
        ent = ctx.enter_context
        xres = ent(nc.sbuf_tensor("xres", [128, TC, D], F32))
        idn = ent(nc.sbuf_tensor("idn_sb", [128, 128], BF16))
        ones = ent(nc.sbuf_tensor("ones_sb", [128, 128], F32))
        lamcol = ent(nc.sbuf_tensor("lamcol", [128, 1], F32))
        sublncol = ent(nc.sbuf_tensor("sublncol", [128, 1], F32))
        stats = ent(nc.sbuf_tensor("stats", [128, 18], F32))
        maskp = ent(nc.sbuf_tensor("maskp", [128, 1], I32))
        masks = ent(nc.sbuf_tensor("masks", [128, 1], I32))
        two16 = ent(nc.sbuf_tensor("two16", [128, 1], I16))
        eps5col = ent(nc.sbuf_tensor("eps5col", [128, 1], F32))
        bigscr = ent(nc.sbuf_tensor("bigscr", [128, 1024], F32))
        rowscr = ent(nc.sbuf_tensor("rowscr", [128, 1024], F32))
        Q = {}

        block = ent(nc.Block())

        class BB:
            pass
        b = BB()
        for name in ["s_ld", "s_cast", "s_cast1", "s_sp", "s_pe", "s_act", "s_dve",
                     "s_gp", "s_st", "s_bar", "s_lw", "s_lst", "s_lwd"]:
            setattr(b, name, Sem(ctx, nc, name))

        st = {"qdve": 0, "qgp": 0, "qact_t2": 0, "wld": 0, "pend": None,
              "slot": 0, "wd_ev": 0, "gu_ev": 0}

        def barrier():
            toks = [(b.s_pe.h, b.s_pe.n), (b.s_act.h, b.s_act.n),
                    (b.s_dve.h, b.s_dve.n), (b.s_gp.h, b.s_gp.n)]
            base = b.s_bar.n
            for sec in [block.sync, block.tensor, block.vector, block.scalar,
                        block.gpsimd]:
                def _f(eng):
                    for h, n in toks:
                        if n:
                            eng.wait_ge(h, n)
                    b.s_bar.inc(eng.nop(), 1)
                sec(_f)
            target = base + 5
            for sec in [block.sync, block.tensor, block.vector, block.scalar,
                        block.gpsimd]:
                sec(lambda eng: eng.wait_ge(b.s_bar.h, target))

        def ph0_sync(s):
            for t in range(TC):
                b.s_ld.inc(s.dma_start(out=xres[:, t, :],
                                       in_=x_d[128 * t:128 * (t + 1), :]), 16)
            b.s_ld.inc(s.dma_start(out=idn[:], in_=idn_d[:]), 16)
            b.s_ld.inc(s.dma_start(out=ones[:], in_=ones_d[:]), 16)
            b.s_ld.inc(s.dma_start(out=rowscr[0:1, 0:128], in_=lam_d[:, :]), 16)
            b.s_ld.inc(s.dma_start(out=sublncol[0:64, :], in_=subln_d[:, :]), 16)
        block.sync(ph0_sync)
        tok_ph0 = b.s_ld.n

        def ph0_vec(v):
            v.wait_ge(b.s_ld.h, tok_ph0)
            v.memset(maskp[:], 0x7FFFFFFF)
            v.memset(masks[:], -2147483648)
            t = b.s_dve.inc(v.memset(two16[:], 2))
            v.wait_ge(b.s_dve.h, t)
            t = b.s_dve.inc(v.memset(eps5col[:], 1e-5))
            v.wait_ge(b.s_dve.h, t)
            t = b.s_dve.inc(v.tensor_single_scalar(sublncol[0:64, :], sublncol[0:64, :],
                                                   1.0 - LAM0, ALU.mult))
            v.wait_ge(b.s_dve.h, t)
        def ph0_gate(s):
            s.wait_ge(b.s_ld.h, tok_ph0)
        block.sync(ph0_gate)
        barrier()
        block.vector(ph0_vec)

        def flush_q():
            pe = st.get("pend")
            if pe is None:
                return st["qdve"]
            st["pend"] = None
            p, F, src, dst = pe["p"], pe["F"], pe["src"], pe["dst"]
            BL = F // 16
            shq = Q["sh2"][p] if pe["halve"] else Q["sh"][p]
            tok = {}

            def vsec2(v, pe=pe, p=p, F=F, BL=BL, shq=shq, tok=tok, src=src, dst=dst):
                v.wait_ge(b.s_gp.h, pe["gp"])
                dw = pe["dst_war"]() if callable(pe["dst_war"]) else pe["dst_war"]
                if dw is not None:
                    v.wait_ge(dw[0], dw[1])
                xn2 = Q["xn2"][p][:, :F]
                t2 = Q["t2s"][:, :F]
                b1h = Q["b1h"][:, :F]
                v.tensor_single_scalar(t2.bitcast(I32), xn2.bitcast(I32),
                                       maskp[:], ALU.bitwise_and)
                v.tensor_single_scalar(Q["A"][:, :F], t2, 4.0, ALU.min)
                v.tensor_scalar(b1h, t2, 0.5, -2.0, ALU.mult, ALU.add)
                v.tensor_scalar(Q["B"][:, :F], b1h, 0.0, 2.0, ALU.max, ALU.min)
                v.tensor_scalar(b1h, t2, -9.5, 0.0, ALU.add, ALU.max)
                v.tensor_single_scalar(Q["C"][:, :F], b1h, 1.0, ALU.min)
                v.scalar_tensor_tensor(Q["C"][:, :F], Q["C"][:, :F], two16[:],
                                       Q["B"][:, :F], ALU.mult, ALU.add)
                v.scalar_tensor_tensor(Q["B"][:, :F], Q["C"][:, :F], two16[:],
                                       Q["A"][:, :F], ALU.mult, ALU.add)
                shb = shq[:, :BL].unsqueeze(2).broadcast_to([128, BL, 16])
                v.scalar_tensor_tensor(
                    t2.bitcast(I32).rearrange("p (bb k) -> p bb k", k=16),
                    xn2.bitcast(I32).rearrange("p (bb k) -> p bb k", k=16),
                    masks[:], shb.bitcast(I32), ALU.bitwise_and, ALU.bitwise_or)
                tok["q"] = b.s_dve.inc(v.tensor_tensor(dst, Q["B"][:, :F], t2,
                                                       ALU.mult))
            block.vector(vsec2)
            st["qdve"] = tok["q"]
            if pe["post"] is not None:
                pe["post"](tok["q"])
            return tok["q"]

        def emit_quant(src, dst, F, halve=False, dst_war=None, src_ready=None,
                       post=None):
            BL = F // 16
            p = st["slot"] & 1
            st["slot"] += 1
            tok = {}

            def vsec1(v, p=p):
                if src_ready is not None:
                    v.wait_ge(src_ready[0], src_ready[1])
                sv = src.rearrange("p (bb k) -> p bb k", k=16)
                t = b.s_dve.inc(v.tensor_reduce(
                    Q["m"][:, :BL], sv, AX.X, ALU.max, apply_absolute_value=True))
                v.wait_ge(b.s_dve.h, t)
                t = b.s_dve.inc(v.tensor_scalar(
                    Q["sh"][p][:, :BL], Q["m"][:, :BL], 1.0 / 12.0, 5e-13,
                    ALU.mult, ALU.max))
                v.wait_ge(b.s_dve.h, t)
                tok["r2"] = b.s_dve.inc(v.reciprocal(Q["r2"][p][:, :BL],
                                                     Q["sh"][p][:, :BL]))
                if halve:
                    tok["r2"] = b.s_dve.inc(v.tensor_single_scalar(
                        Q["sh2"][p][:, :BL], Q["sh"][p][:, :BL], 0.5, ALU.mult))
                v.wait_ge(b.s_dve.h, tok["r2"])
            block.vector(vsec1)

            def gsec(g, p=p):
                g.wait_ge(b.s_dve.h, tok["r2"])
                if src_ready is not None:
                    g.wait_ge(src_ready[0], src_ready[1])
                if st["qdve"]:
                    g.wait_ge(b.s_dve.h, st["qdve"])
                r2b = Q["r2"][p][:, :BL].unsqueeze(2).broadcast_to([128, BL, 16])
                xn = Q["xn2"][p]
                tok["gp"] = b.s_gp.inc(g.tensor_tensor(
                    xn[:, :F].rearrange("p (bb k) -> p bb k", k=16),
                    src.rearrange("p (bb k) -> p bb k", k=16), r2b, ALU.mult))
            block.gpsimd(gsec)
            st["qgp"] = tok["gp"]
            st["qr2"] = tok["r2"]

            flush_q()
            st["pend"] = dict(p=p, F=F, src=src, dst=dst, halve=halve,
                              dst_war=dst_war, post=post, gp=tok["gp"])
            return tok

        def emit_ln(src, dst, dst_war=None):
            tok = {}

            def vsec(v):
                t = b.s_dve.inc(v.bn_stats(stats[:, 0:6], src[:, 0:512]))
                t = b.s_dve.inc(v.bn_stats(stats[:, 8:14], src[:, 512:1024]))
                v.wait_ge(b.s_dve.h, t)
                t = b.s_dve.inc(v.tensor_copy(stats[:, 6:8], stats[:, 8:10]))
                v.wait_ge(b.s_dve.h, t)
                t = b.s_dve.inc(v.tensor_copy(stats[:, 8:12], stats[:, 10:14]))
                v.wait_ge(b.s_dve.h, t)
                tok["mv"] = b.s_dve.inc(v.bn_aggr(stats[:, 14:16], stats[:, 0:12]))
                v.wait_ge(b.s_dve.h, tok["mv"])
            block.vector(vsec)

            def asec2(sc):
                sc.wait_ge(b.s_dve.h, tok["mv"])
                t = b.s_act.inc(sc.activation(stats[:, 16:17], stats[:, 15:16], AF.Ln,
                                              bias=eps5col[:, 0:1]))
                sc.wait_ge(b.s_act.h, t)
                tok["rstd"] = b.s_act.inc(sc.activation(stats[:, 16:17], stats[:, 16:17],
                                                        AF.Exp, scale=-0.5))
            block.scalar(asec2)

            def vsec2(v):
                v.wait_ge(b.s_act.h, tok["rstd"])
                t = b.s_dve.inc(v.tensor_scalar(stats[:, 17:18], stats[:, 14:15],
                                                stats[:, 16:17], -1.0,
                                                ALU.mult, ALU.mult))
                v.wait_ge(b.s_dve.h, t)
                if st["qgp"]:
                    v.wait_ge(b.s_gp.h, st["qgp"])
                if dst_war is not None:
                    v.wait_ge(dst_war[0], dst_war[1])
                tok["out"] = b.s_dve.inc(v.tensor_scalar(dst, src, stats[:, 16:17],
                                                         stats[:, 17:18],
                                                         ALU.mult, ALU.add))
            block.vector(vsec2)
            return tok["out"]

        def ffn(first):
            sfx = "a" if first else "b"
            fctx = ExitStack()
            with fctx:
                fent = fctx.enter_context
                wld = fent(nc.sbuf_tensor("wld" + sfx, [128, HID], F32))
                Q["xn2"] = [fent(nc.sbuf_tensor(f"qs_xn2{i}" + sfx, [128, 1024], F32))
                            for i in range(2)]
                Q["t2s"] = bigscr
                Q["b1h"] = fent(nc.sbuf_tensor("qs_b1h" + sfx, [128, 1024], F32))
                Q["A"] = fent(nc.sbuf_tensor("qs_A" + sfx, [128, 1024], I16))
                Q["B"] = fent(nc.sbuf_tensor("qs_B" + sfx, [128, 1024], I16))
                Q["C"] = fent(nc.sbuf_tensor("qs_C" + sfx, [128, 1024], I16))
                Q["m"] = fent(nc.sbuf_tensor("qs_m" + sfx, [128, 64], F32))
                Q["sh"] = [fent(nc.sbuf_tensor(f"qs_sh{i}" + sfx, [128, 64], F32))
                           for i in range(2)]
                Q["sh2"] = [fent(nc.sbuf_tensor(f"qs_sh2{i}" + sfx, [128, 64], F32))
                            for i in range(2)]
                Q["r2"] = [fent(nc.sbuf_tensor(f"qs_r2{i}" + sfx, [128, 64], F32))
                           for i in range(2)]
                ln_sb = wld[:, 0:D]
                wq_chunk = fent(nc.sbuf_tensor("wq_chunk" + sfx, [128, 1024], BF16))
                wgstrip = fent(nc.sbuf_tensor("wgstrip" + sfx, [128, DC, 512], BF16))
                wustrip = fent(nc.sbuf_tensor("wustrip" + sfx, [128, DC, 512], BF16))
                wdqT = fent(nc.sbuf_tensor("wdqT" + sfx, [128, HC, D], BF16))
                xqT = fent(nc.sbuf_tensor("xqT" + sfx, [128, DC, 512], BF16))
                hqT = fent(nc.sbuf_tensor("hqT" + sfx, [128, HC, 512], BF16))
                xq_sb = fent(nc.sbuf_tensor("xq_sb" + sfx, [128, D], BF16))
                silu_sb = [fent(nc.sbuf_tensor(f"silu{i}" + sfx, [128, 512], F32)) for i in range(2)]
                h_sb = [fent(nc.sbuf_tensor(f"h{i}" + sfx, [128, 512], F32)) for i in range(2)]
                hq_sb = [fent(nc.sbuf_tensor(f"hq{i}" + sfx, [128, 512], BF16)) for i in range(2)]
                psG = [fent(nc.psum_tensor(f"psG{i}" + sfx, [128, 512], F32)) for i in range(2)]
                psU = [fent(nc.psum_tensor(f"psU{i}" + sfx, [128, 512], F32)) for i in range(2)]
                psD = [fent(nc.psum_tensor(f"psD{i}" + sfx, [128, 512], F32)) for i in range(2)]
                psT = [fent(nc.psum_tensor(f"psT{i}" + sfx, [128, 128], BF16)) for i in range(2)]

                fst = {"tp": 0, "evac0": 0, "evac1": 0, "strip_pe": 0,
                       "sup_pe": 0, "silu_rd0": 0, "silu_rd1": 0,
                       "hpipe0": 0, "hpipe1": 0, "dres0": 0, "dres1": 0,
                       "wq_pe": 0}

                def emit_tp(src, dst, pe_wait=None, evac_war=()):
                    i = fst["tp"] % 2
                    fst["tp"] += 1
                    tk = {}
                    wars = [w for w in (evac_war if isinstance(evac_war, (list, tuple)) and
                                        (not evac_war or isinstance(evac_war[0], (list, tuple)))
                                        else [evac_war]) if w]

                    def tsec(t):
                        if pe_wait is not None:
                            t.wait_ge(pe_wait[0], pe_wait[1])
                        if fst[f"evac{i}"]:
                            t.wait_ge(b.s_dve.h, fst[f"evac{i}"])
                        tk["tp"] = b.s_pe.inc(t.transpose(psT[i][:], src, idn[:]))
                    block.tensor(tsec)

                    def vsec(v):
                        v.wait_ge(b.s_pe.h, tk["tp"])
                        for w in wars:
                            v.wait_ge(w[0], w[1])
                        tk["ev"] = b.s_dve.inc(v.tensor_copy(dst, psT[i][:]))
                    block.vector(vsec)
                    fst[f"evac{i}"] = tk["ev"]
                    return tk["ev"]

                def wq_war():
                    return (b.s_pe.h, fst["wq_pe"]) if fst["wq_pe"] else None

                def quant_w_chunk(dram_rows, nf, dst, halve, post=None):
                    def ssec(s):
                        if st["qgp"]:
                            s.wait_ge(b.s_gp.h, st["qgp"])
                        if st.get("qr2"):
                            s.wait_ge(b.s_dve.h, st["qr2"])
                        st["wld"] = b.s_lw.inc(
                            s.dma_start(out=wld[:, :nf], in_=dram_rows), 16)
                    block.sync(ssec)

                    def vw(v):
                        v.wait_ge(b.s_lw.h, st["wld"])
                    block.vector(vw)
                    off = 0
                    while off < nf:
                        piece = min(1024, nf - off)
                        qret = emit_quant(wld[:, off:off + piece],
                                          dst[:, off:off + piece], piece,
                                          halve=halve, dst_war=wq_war,
                                          post=post(off, piece) if post else None)
                        st["w_gp"] = qret["gp"]
                        st["w_r2"] = qret["r2"]
                        off += piece

                if first:
                    for c in range(DC):
                        def ssec(s, c=c):
                            if st["qgp"]:
                                s.wait_ge(b.s_gp.h, st["qgp"])
                            if st.get("qr2"):
                                s.wait_ge(b.s_dve.h, st["qr2"])
                            st["wld"] = b.s_lw.inc(s.dma_start(
                                out=wld[:, :HID],
                                in_=wd_d[128 * c:128 * (c + 1), :]), 16)
                        block.sync(ssec)

                        def vw(v):
                            v.wait_ge(b.s_lw.h, st["wld"])
                        block.vector(vw)
                        for off, piece in ((0, 1024), (1024, 1024), (2048, 768)):
                            def mkpost(off=off, piece=piece, c=c):
                                def post(qtok):
                                    for p2 in range(piece // 128):
                                        cc = (off + 128 * p2) // 128
                                        st["wd_ev"] = emit_tp(
                                            wq_chunk[:, 128 * p2:128 * (p2 + 1)],
                                            wdqT[:, cc, 128 * c:128 * (c + 1)],
                                            pe_wait=(b.s_dve.h, qtok))
                                    fst["wq_pe"] = b.s_pe.n
                                return post
                            qret = emit_quant(wld[:, off:off + piece],
                                              wq_chunk[:, :piece], piece,
                                              halve=True, dst_war=wq_war,
                                              post=mkpost())
                            st["w_gp"] = qret["gp"]
                            st["w_r2"] = qret["r2"]
                    flush_q()

                    def spill_wd(g, ev=st["wd_ev"]):
                        g.wait_ge(b.s_dve.h, ev)
                        if DEBUG_DUMP and "ffn" in DEBUG_SEL:
                            b.s_sp.inc(g.dma_start(out=dbg["wdq0"], in_=wdqT[:, 0, :]), 16)
                        for cc in range(HC):
                            b.s_sp.inc(g.dma_start(out=wdq_s[cc, :, :],
                                                   in_=wdqT[:, cc, :]), 16)
                    block.gpsimd(spill_wd)
                else:
                    def reload_wd(s):
                        s.wait_ge(b.s_sp.h, b.s_sp.n)
                        for cc in range(HC):
                            b.s_lwd.inc(s.dma_start(out=wdqT[:, cc, :],
                                                    in_=wdq_s[cc, :, :]), 16)
                        st["wd_re"] = b.s_lwd.n
                    block.sync(reload_wd)

                    def wait_wd(t):
                        t.wait_ge(b.s_lwd.h, st["wd_re"])
                    block.tensor(wait_wd)

                ln_slots = [wld[:, 0:D], wld[:, 1024:1024 + D]]
                for S in range(NSUP):
                    tts = [4 * S + i for i in range(4)]
                    for k, tt in enumerate(tts):
                        ln_dst = ln_slots[k & 1]
                        lntok = emit_ln(xres[:, tt, :], ln_dst)
                        if DEBUG_DUMP and "ffn" in DEBUG_SEL and first and tt == 2:
                            def dmp(g, lntok=lntok, ln_dst=ln_dst):
                                g.wait_ge(b.s_dve.h, lntok)
                                b.s_sp.inc(g.dma_start(out=dbg["ln"], in_=ln_dst), 16)
                            block.gpsimd(dmp)

                        def mkpost(k=k, tt=tt):
                            def post(qtok):
                                if (DEBUG_DUMP and "ffn" in DEBUG_SEL and first
                                        and tt == 2):
                                    def dmp2(g, qtok=qtok):
                                        g.wait_ge(b.s_dve.h, qtok)
                                        b.s_sp.inc(g.dma_start(out=dbg["xq"],
                                                               in_=xq_sb[:]), 16)
                                    block.gpsimd(dmp2)
                                ev_war = ([(b.s_pe.h, fst["sup_pe"])]
                                          if fst["sup_pe"] else [])
                                for c in range(DC):
                                    emit_tp(xq_sb[:, 128 * c:128 * (c + 1)],
                                            xqT[:, c, 128 * k:128 * k + 128],
                                            pe_wait=(b.s_dve.h, qtok),
                                            evac_war=ev_war)
                                fst["xq_pe"] = b.s_pe.n
                            return post
                        emit_quant(ln_dst, xq_sb[:], D, halve=False,
                                   dst_war=lambda: ((b.s_pe.h, fst["xq_pe"])
                                                    if fst.get("xq_pe") else None),
                                   post=mkpost())
                    flush_q()

                    for si, (soff, slen) in enumerate(STRIPS):
                        if first and S == 0:
                            for wdram, wstrip in ((wg_d, wgstrip), (wu_d, wustrip)):
                                for c2 in range(slen // 128):
                                    r0 = soff + 128 * c2

                                    def mkwpost(c2=c2, wstrip=wstrip):
                                        def factory(off, piece):
                                            def post(qtok):
                                                ev_war = []
                                                if fst["strip_pe"]:
                                                    ev_war.append(
                                                        (b.s_pe.h, fst["strip_pe"]))
                                                if fst.get("spill_tok"):
                                                    ev_war.append(
                                                        (b.s_sp.h, fst["spill_tok"]))
                                                for cc in range(DC):
                                                    st["gu_ev"] = emit_tp(
                                                        wq_chunk[:, 128 * cc:
                                                                 128 * (cc + 1)],
                                                        wstrip[:, cc, 128 * c2:
                                                               128 * c2 + 128],
                                                        pe_wait=(b.s_dve.h, qtok),
                                                        evac_war=ev_war)
                                                fst["wq_pe"] = b.s_pe.n
                                            return post
                                        return factory
                                    quant_w_chunk(wdram[r0:r0 + 128, :], D,
                                                  wq_chunk[:, :D], halve=False,
                                                  post=mkwpost())
                            flush_q()
                            last_ev = st["gu_ev"]

                            def spill_strips(g, soff=soff, slen=slen, ev=last_ev):
                                g.wait_ge(b.s_dve.h, ev)
                                for wspill, wstrip in ((wgq_s, wgstrip), (wuq_s, wustrip)):
                                    for cc in range(DC):
                                        b.s_sp.inc(g.dma_start(
                                            out=wspill[cc, :, soff:soff + slen],
                                            in_=wstrip[:, cc, :slen]), 16)
                            block.gpsimd(spill_strips)
                            fst["spill_tok"] = b.s_sp.n

                            def twait(t, ev=last_ev):
                                t.wait_ge(b.s_dve.h, ev)
                            block.tensor(twait)
                        else:
                            def reload_strips(s, soff=soff, slen=slen):
                                if fst["strip_pe"]:
                                    s.wait_ge(b.s_pe.h, fst["strip_pe"])
                                s.wait_ge(b.s_sp.h, b.s_sp.n)
                                for wspill, wstrip in ((wgq_s, wgstrip), (wuq_s, wustrip)):
                                    for cc in range(DC):
                                        b.s_lst.inc(s.dma_start(
                                            out=wstrip[:, cc, :slen],
                                            in_=wspill[cc, :, soff:soff + slen]), 16)
                                st["strip_re"] = b.s_lst.n
                            block.sync(reload_strips)

                            def twait2(t):
                                t.wait_ge(b.s_lst.h, st["strip_re"])
                            block.tensor(twait2)

                        def do_hjob(job):
                            par, k, slen2, soff2, tok = (job["par"], job["k"],
                                                         job["slen"], job["soff"],
                                                         job["tok"])

                            def asec(sc, par=par, slen2=slen2, tok=tok):
                                sc.wait_ge(b.s_pe.h, tok["g"])
                                if fst[f"silu_rd{par}"]:
                                    sc.wait_ge(b.s_dve.h, fst[f"silu_rd{par}"])
                                tok["silu"] = b.s_act.inc(sc.activation(
                                    silu_sb[par][:, :slen2], psG[par][:, :slen2],
                                    AF.Silu))
                            block.scalar(asec)
                            fst[f"sfree{par}"] = tok["silu"]

                            def vsec(v, par=par, slen2=slen2, tok=tok):
                                v.wait_ge(b.s_act.h, tok["silu"])
                                v.wait_ge(b.s_pe.h, tok["u"])
                                if fst.get(f"hgp{par}"):
                                    v.wait_ge(b.s_gp.h, fst[f"hgp{par}"])
                                tok["h"] = b.s_dve.inc(v.tensor_tensor(
                                    h_sb[par][:, :slen2], silu_sb[par][:, :slen2],
                                    psU[par][:, :slen2], ALU.mult))
                            block.vector(vsec)
                            fst[f"silu_rd{par}"] = tok["h"]
                            if job["dump"]:
                                def dmph(g, par=par, tok=tok):
                                    g.wait_ge(b.s_dve.h, tok["h"])
                                    b.s_sp.inc(g.dma_start(out=dbg["h"],
                                                           in_=h_sb[par][:, :]), 16)
                                    b.s_sp.inc(g.dma_start(out=dbg["up"],
                                                           in_=silu_sb[par][:, :]), 16)
                                block.gpsimd(dmph)

                            def mkhpost(par=par, k=k, soff2=soff2, slen2=slen2,
                                        dump=job["dump"]):
                                def post(qtok):
                                    fst[f"hpipe{par}"] = qtok
                                    if dump:
                                        def dmphq(g, par=par, qtok=qtok):
                                            g.wait_ge(b.s_dve.h, qtok)
                                            b.s_sp.inc(g.dma_start(
                                                out=dbg["hq"],
                                                in_=hq_sb[par][:, :]), 16)
                                        block.gpsimd(dmphq)
                                    ev_war = ([(b.s_pe.h, fst["sup_pe"])]
                                              if fst["sup_pe"] else [])
                                    for p2 in range(slen2 // 128):
                                        cc = (soff2 + 128 * p2) // 128
                                        emit_tp(hq_sb[par][:, 128 * p2:128 * p2 + 128],
                                                hqT[:, cc, 128 * k:128 * k + 128],
                                                pe_wait=(b.s_dve.h, qtok),
                                                evac_war=ev_war)
                                    fst[f"hq_pe{par}"] = b.s_pe.n
                                return post
                            qret = emit_quant(h_sb[par][:, :slen2],
                                              hq_sb[par][:, :slen2],
                                              slen2, halve=False,
                                              dst_war=lambda par=par: (
                                                  (b.s_pe.h, fst[f"hq_pe{par}"])
                                                  if fst.get(f"hq_pe{par}") else None),
                                              post=mkhpost())
                            fst[f"hgp{par}"] = qret["gp"]

                        for k, tt in enumerate(tts):
                            par = k % 2
                            tok = {}

                            def tsec(t, par=par, k=k, slen=slen, tok=tok):
                                if fst.get(f"sfree{par}"):
                                    t.wait_ge(b.s_act.h, fst[f"sfree{par}"])
                                if fst[f"silu_rd{par}"]:
                                    t.wait_ge(b.s_dve.h, fst[f"silu_rd{par}"])
                                ig = iu = None
                                for cc in range(DC):
                                    ig = t.matmul(psG[par][:, :slen],
                                                  xqT[:, cc, 128 * k:128 * k + 128],
                                                  wgstrip[:, cc, :slen],
                                                  start=(cc == 0), stop=(cc == DC - 1))
                                    iu = t.matmul(psU[par][:, :slen],
                                                  xqT[:, cc, 128 * k:128 * k + 128],
                                                  wustrip[:, cc, :slen],
                                                  start=(cc == 0), stop=(cc == DC - 1))
                                tok["g"] = b.s_pe.inc(ig)
                                tok["u"] = b.s_pe.inc(iu)
                            block.tensor(tsec)
                            fst["strip_pe"] = b.s_pe.n
                            if fst.get("hjob") is not None:
                                do_hjob(fst["hjob"])
                            fst["hjob"] = dict(
                                par=par, k=k, slen=slen, soff=soff, tok=tok,
                                dump=(DEBUG_DUMP and "ffn" in DEBUG_SEL and first
                                      and S == 0 and si == 0 and k == 2))

                    if fst.get("hjob") is not None:
                        do_hjob(fst["hjob"])
                        fst["hjob"] = None
                    flush_q()
                    for k, tt in enumerate(tts):
                        tok = {}

                        def tsec(t, k=k, tok=tok):
                            for f in range(2):
                                if fst[f"dres{f}"]:
                                    t.wait_ge(b.s_dve.h, fst[f"dres{f}"])
                                im = None
                                for cc in range(HC):
                                    im = t.matmul(psD[f][:, :],
                                                  hqT[:, cc, 128 * k:128 * k + 128],
                                                  wdqT[:, cc, 512 * f:512 * f + 512],
                                                  start=(cc == 0), stop=(cc == HC - 1))
                                tok[f"d{f}"] = b.s_pe.inc(im)
                        block.tensor(tsec)

                        def vsec(v, tt=tt, tok=tok):
                            for f in range(2):
                                v.wait_ge(b.s_pe.h, tok[f"d{f}"])
                                fst[f"dres{f}"] = b.s_dve.inc(v.tensor_add(
                                    xres[:, tt, 512 * f:512 * f + 512],
                                    xres[:, tt, 512 * f:512 * f + 512], psD[f][:, :]))
                        block.vector(vsec)
                    fst["sup_pe"] = b.s_pe.n

            barrier()

        def attention():
            actx = ExitStack()
            with actx:
                aent = actx.enter_context
                qT = aent(nc.sbuf_tensor("qT", [128, DC, T], BF16))
                kT = aent(nc.sbuf_tensor("kT", [128, DC, T], BF16))
                v_sb = aent(nc.sbuf_tensor("v_sb", [128, 16, 1040], BF16))
                ast = {"tp": 0, "evac0": 0, "evac1": 0}

                pctx = ExitStack()
                with pctx:
                    pent = pctx.enter_context
                    ln2T = pent(nc.sbuf_tensor("ln2T", [128, DC, T], BF16))
                    ln2c = pent(nc.sbuf_tensor("ln2c", [128, D], BF16))
                    wproj = pent(nc.sbuf_tensor("wproj", [128, DC, 128], BF16))
                    vtmp = pent(nc.sbuf_tensor("vtmp", [128, 512], BF16))
                    psP = [pent(nc.psum_tensor(f"psP{i}", [128, 512], F32)) for i in range(2)]
                    psTT = [pent(nc.psum_tensor(f"psTT{i}", [128, 128], BF16)) for i in range(2)]

                    def emit_tp2(src, dst, pe_wait=None, evac_war=None):
                        i = ast["tp"] % 2
                        ast["tp"] += 1
                        tk = {}

                        def tsec(t):
                            if pe_wait is not None:
                                t.wait_ge(pe_wait[0], pe_wait[1])
                            if ast[f"evac{i}"]:
                                t.wait_ge(b.s_dve.h, ast[f"evac{i}"])
                            tk["tp"] = b.s_pe.inc(t.transpose(psTT[i][:], src, idn[:]))
                        block.tensor(tsec)

                        def vsec(v):
                            v.wait_ge(b.s_pe.h, tk["tp"])
                            if evac_war is not None:
                                v.wait_ge(evac_war[0], evac_war[1])
                            tk["ev"] = b.s_dve.inc(v.tensor_copy(dst, psTT[i][:]))
                        block.vector(vsec)
                        ast[f"evac{i}"] = tk["ev"]
                        return tk["ev"]

                    lamtok = {}

                    def lam_v(v):
                        t = b.s_dve.inc(v.tensor_mul(rowscr[0:1, 128:160],
                                                     rowscr[0:1, 0:32],
                                                     rowscr[0:1, 32:64]))
                        v.wait_ge(b.s_dve.h, t)
                        t = b.s_dve.inc(v.tensor_mul(rowscr[0:1, 160:192],
                                                     rowscr[0:1, 64:96],
                                                     rowscr[0:1, 96:128]))
                        v.wait_ge(b.s_dve.h, t)
                        t = b.s_dve.inc(v.reduce_sum(rowscr[0:1, 192:193],
                                                     rowscr[0:1, 128:160], AX.X))
                        v.wait_ge(b.s_dve.h, t)
                        t = b.s_dve.inc(v.reduce_sum(rowscr[0:1, 193:194],
                                                     rowscr[0:1, 160:192], AX.X))
                        v.wait_ge(b.s_dve.h, t)
                        lamtok["sums"] = t
                    block.vector(lam_v)

                    def lam_a(sc):
                        sc.wait_ge(b.s_dve.h, lamtok["sums"])
                        lamtok["exps"] = b.s_act.inc(sc.activation(
                            rowscr[0:1, 194:196], rowscr[0:1, 192:194], AF.Exp))
                    block.scalar(lam_a)

                    def lam_v2(v):
                        v.wait_ge(b.s_act.h, lamtok["exps"])
                        t = b.s_dve.inc(v.tensor_sub(rowscr[0:1, 196:197],
                                                     rowscr[0:1, 194:195],
                                                     rowscr[0:1, 195:196]))
                        v.wait_ge(b.s_dve.h, t)
                        lamtok["lam"] = b.s_dve.inc(v.tensor_single_scalar(
                            rowscr[0:1, 196:197], rowscr[0:1, 196:197], LAM0, ALU.add))
                        v.wait_ge(b.s_dve.h, lamtok["lam"])
                    block.vector(lam_v2)

                    def lam_t(t):
                        t.wait_ge(b.s_dve.h, lamtok["lam"])
                        lamtok["bc"] = b.s_pe.inc(t.matmul(
                            psP[0][:, 0:1], ones[0:1, :], rowscr[0:1, 196:197],
                            start=True, stop=True))
                    block.tensor(lam_t)

                    def lam_v3(v):
                        v.wait_ge(b.s_pe.h, lamtok["bc"])
                        t = b.s_dve.inc(v.tensor_copy(lamcol[:], psP[0][:, 0:1]))
                        v.wait_ge(b.s_dve.h, t)
                    block.vector(lam_v3)

                    for tt in range(TC if ATTN_STAGE >= 2 else 0):
                        war = (b.s_pe.h, ast.get("ln2c_pe", 0)) if ast.get("ln2c_pe") else None
                        lntok2 = emit_ln(xres[:, tt, :], ln2c[:], dst_war=war)
                        for c in range(DC):
                            emit_tp2(ln2c[:, 128 * c:128 * (c + 1)],
                                     ln2T[:, c, 128 * tt:128 * tt + 128],
                                     pe_wait=(b.s_dve.h, lntok2))
                        ast["ln2c_pe"] = b.s_pe.n
                    ast["ln2T_ev"] = b.s_dve.n

                    def project_fm(w_dram, dstT):
                        for fo in range(DC):
                            ctok = {}

                            def gsec(g, fo=fo, ctok=ctok):
                                if ast.get("wproj_pe"):
                                    g.wait_ge(b.s_pe.h, ast["wproj_pe"])
                                for e in range(DC):
                                    ctok[e] = b.s_cast.inc(g.dma_start(
                                        out=wproj[:, e, :],
                                        in_=w_dram[128 * e:128 * (e + 1),
                                                   128 * fo:128 * (fo + 1)]), 16)
                            block.gpsimd(gsec)

                            for ts4 in range(4):
                                par = ts4 % 2
                                mtok = {}

                                def tsec(t, ts4=ts4, par=par, ctok=ctok, mtok=mtok):
                                    t.wait_ge(b.s_cast.h, ctok[DC - 1])
                                    t.wait_ge(b.s_dve.h, ast["ln2T_ev"])
                                    if ast.get(f"pevac{par}"):
                                        t.wait_ge(b.s_act.h, ast[f"pevac{par}"])
                                    im = None
                                    for e in range(DC):
                                        im = t.matmul(psP[par][:], wproj[:, e, :],
                                                      ln2T[:, e, 512 * ts4:512 * ts4 + 512],
                                                      start=(e == 0), stop=(e == DC - 1))
                                    mtok["m"] = b.s_pe.inc(im)
                                block.tensor(tsec)
                                ast["wproj_pe"] = b.s_pe.n

                                def asecE(sc, fo=fo, ts4=ts4, par=par, mtok=mtok, dstT=dstT):
                                    sc.wait_ge(b.s_pe.h, mtok["m"])
                                    ast[f"pevac{par}"] = b.s_act.inc(sc.copy(
                                        dstT[:, fo, 512 * ts4:512 * ts4 + 512],
                                        psP[par][:]))
                                block.scalar(asecE)

                    if ATTN_STAGE >= 3:
                        project_fm(wq_d, qT)
                        project_fm(wk_d, kT)

                    def vv(v):
                        t = b.s_dve.inc(v.memset(
                            v_sb[:].rearrange("p s (hh c) -> p s hh c", c=65)[:, :, :, 64:65],
                            1.0))
                        v.wait_ge(b.s_dve.h, t)
                    block.vector(vv)

                    for dc_i in range(DC if ATTN_STAGE >= 4 else 0):
                        ctok = {}

                        def gsec(g, dc_i=dc_i, ctok=ctok):
                            if ast.get("wproj_pe"):
                                g.wait_ge(b.s_pe.h, ast["wproj_pe"])
                            for e in range(DC):
                                ctok[e] = b.s_cast.inc(g.dma_start(
                                    out=wproj[:, e, :],
                                    in_=wv_d[128 * e:128 * (e + 1),
                                             128 * dc_i:128 * (dc_i + 1)]), 16)
                        block.gpsimd(gsec)

                        for ts4 in range(4):
                            par = ts4 % 2
                            mtok = {}

                            def tsec(t, ts4=ts4, par=par, ctok=ctok, mtok=mtok):
                                t.wait_ge(b.s_cast.h, ctok[DC - 1])
                                t.wait_ge(b.s_dve.h, ast["ln2T_ev"])
                                if ast.get(f"pevac{par}"):
                                    t.wait_ge(b.s_act.h, ast[f"pevac{par}"])
                                im = None
                                for e in range(DC):
                                    im = t.matmul(psP[par][:], wproj[:, e, :],
                                                  ln2T[:, e, 512 * ts4:512 * ts4 + 512],
                                                  start=(e == 0), stop=(e == DC - 1))
                                mtok["m"] = b.s_pe.inc(im)
                            block.tensor(tsec)
                            ast["wproj_pe"] = b.s_pe.n

                            def asecE(sc, par=par, mtok=mtok):
                                sc.wait_ge(b.s_pe.h, mtok["m"])
                                if ast.get("vtmp_pe"):
                                    sc.wait_ge(b.s_pe.h, ast["vtmp_pe"])
                                mtok["ev"] = b.s_act.inc(sc.copy(vtmp[:], psP[par][:]))
                            block.scalar(asecE)
                            ast[f"pevac{par}"] = mtok["ev"]

                            for p2 in range(4):
                                sc_i = 4 * ts4 + p2
                                dstv = v_sb[:, sc_i, :].rearrange(
                                    "p (hh c) -> p hh c", c=65)[:, 2 * dc_i:2 * dc_i + 2, 0:64]
                                emit_tp2(vtmp[:, 128 * p2:128 * p2 + 128], dstv,
                                         pe_wait=(b.s_act.h, mtok["ev"]))
                            ast["vtmp_pe"] = b.s_pe.n

                    if DEBUG_DUMP and "proj" in DEBUG_SEL:
                        def dmpp(g_):
                            g_.wait_ge(b.s_dve.h, b.s_dve.n)
                            b.s_sp.inc(g_.dma_start(out=dbg["qt0"], in_=qT[:, 0, 0:512]), 16)
                            b.s_sp.inc(g_.dma_start(out=dbg["ln2t0"], in_=ln2T[:, 0, 0:512]), 16)
                            b.s_sp.inc(g_.dma_start(out=dbg["kt0"], in_=kT[:, 0, :]), 16)
                            b.s_sp.inc(g_.dma_start(out=dbg["v0"], in_=v_sb[:, 0, :]), 16)
                        block.gpsimd(dmpp)
                    barrier()

                sctx = ExitStack()
                with sctx:
                    sent = sctx.enter_context
                    e_sb = sent(nc.sbuf_tensor("e_sb", [128, 4, 2, 512], BF16))
                    nrm = sent(nc.sbuf_tensor("nrm", [128, 16, 512], BF16))
                    scr_o = sent(nc.sbuf_tensor("scr_o", [128, 512], F32))
                    scr_t = sent(nc.sbuf_tensor("scr_t", [128, 512], F32))
                    scr_cr = sent(nc.sbuf_tensor("scr_cr", [128, 512], F32))
                    wob = sent(nc.sbuf_tensor("wob", [128, 4, 512], BF16))
                    wob2 = sent(nc.sbuf_tensor("wob2", [128, 4, 512], BF16))
                    psSc = [sent(nc.psum_tensor(f"psSc{i}", [128, 512], F32)) for i in range(4)]
                    psA = [sent(nc.psum_tensor(f"psA{i}", [128, 512], F32)) for i in range(4)]

                    sst = {}
                    for i in range(4):
                        sst[f"exp{i}"] = 0
                        sst[f"av{i}0"] = 0
                        sst[f"av{i}1"] = 0
                        sst[f"scfree{i}"] = None
                        sst[f"pafree{i}"] = None
                    sst["wo_pe"] = 0
                    sst["nrm_dve"] = 0

                    def guard_v(v, inst):
                        t = b.s_dve.inc(inst)
                        v.wait_ge(b.s_dve.h, t)
                        return t

                    def guard_a(sc, inst):
                        t = b.s_act.inc(inst)
                        sc.wait_ge(b.s_act.h, t)
                        return t

                    def do_avjob(job):
                        def tv(t, job=job):
                            g2, s_i, par = job["g"], job["s"], job["par"]
                            for i in range(4):
                                t.wait_ge(b.s_act.h, job["exp"][i])
                                if s_i == 0 and sst[f"pafree{i}"] is not None:
                                    w = sst[f"pafree{i}"]
                                    t.wait_ge(w[0], w[1])
                                    sst[f"pafree{i}"] = None
                                vj = 2 * g2 + i // 2
                                im = t.matmul(
                                    psA[i][0:65, :],
                                    v_sb[:, s_i, 65 * vj:65 * vj + 65],
                                    e_sb[:, i, job["par"], :],
                                    start=(s_i == 0), stop=(s_i == ATTN_NS - 1))
                                sst[f"av{i}{par}"] = b.s_pe.inc(im)
                                job["avtok"][i] = sst[f"av{i}{par}"]
                        block.tensor(tv)

                    for tau in range(ATTN_NTAU if ATTN_STAGE >= 5 else 0):
                        for g in range(ATTN_NG):
                            avpend = [None]
                            for s_i in range(ATTN_NS):
                                par = s_i % 2
                                qktok = {}

                                def tq(t, g=g, tau=tau, s_i=s_i, qktok=qktok,
                                       prev=avpend[0]):
                                    if prev is not None:
                                        t.wait_ge(b.s_act.h, prev["exp"][3])
                                    if ATTN_SUB == 5:
                                        for i in range(4):
                                            if sst[f"scfree{i}"] is not None:
                                                w = sst[f"scfree{i}"]
                                                t.wait_ge(w[0], w[1])
                                                sst[f"scfree{i}"] = None
                                        for i in range(4):
                                            if i == 2:
                                                t.wait_ge(b.s_pe.h, qktok[1])
                                            t.ldweights(
                                                kT[32*i:32*i+32, g, 128*s_i:128*s_i+128],
                                                tile_position=(32*i, 0))
                                            qktok[i] = b.s_pe.inc(t.matmul(
                                                psSc[i][:],
                                                kT[32*i:32*i+32, g, 128*s_i:128*s_i+128],
                                                qT[32*i:32*i+32, g, 512*tau:512*tau+512],
                                                start=True, stop=True,
                                                tile_position=(32*i, 0)))
                                        return
                                    if ATTN_SUB == -1:
                                        for i in range(3):
                                            t.ldweights(kT[32*i:32*i+32, g, 128*s_i:128*s_i+128],
                                                        tile_position=(32*i, 0))
                                            qktok[i] = b.s_pe.inc(t.matmul(
                                                psSc[i][:],
                                                kT[32*i:32*i+32, g, 128*s_i:128*s_i+128],
                                                qT[32*i:32*i+32, g, 512*tau:512*tau+512],
                                                start=True, stop=True,
                                                tile_position=(32*i, 0)))
                                        qktok[3] = qktok[2]
                                        return
                                    if ATTN_SUB == 0:
                                        qktok[0] = b.s_pe.inc(t.matmul(
                                            psSc[0][:], kT[:, 0, 0:128], kT[:, 0, 0:512],
                                            start=True, stop=True))
                                        for i in range(1, 4):
                                            qktok[i] = qktok[0]
                                        return
                                    if ATTN_SUB == -2:
                                        qktok[0] = b.s_pe.inc(t.matmul(
                                            psSc[0][:], kT[0:32, 0, 0:128], kT[0:32, 0, 0:512],
                                            start=True, stop=True))
                                        for i in range(1, 4):
                                            qktok[i] = qktok[0]
                                        return
                                    if ATTN_SUB == -4:
                                        for i in range(2):
                                            t.ldweights(kT[32*i:32*i+32, 0, 0:128],
                                                        tile_position=(32*i, 0))
                                            qktok[i] = b.s_pe.inc(t.matmul(
                                                psSc[i][:], kT[32*i:32*i+32, 0, 0:128],
                                                kT[32*i:32*i+32, 0, 0:512],
                                                start=True, stop=True,
                                                tile_position=(32*i, 0)))
                                        qktok[2] = qktok[1]; qktok[3] = qktok[1]
                                        return
                                    if ATTN_SUB == -3:
                                        t.ldweights(kT[32:64, 0, 0:128], tile_position=(32, 0))
                                        qktok[0] = b.s_pe.inc(t.matmul(
                                            psSc[0][:], kT[32:64, 0, 0:128], kT[32:64, 0, 0:512],
                                            start=True, stop=True, tile_position=(32, 0)))
                                        for i in range(1, 4):
                                            qktok[i] = qktok[0]
                                        return
                                    for i in range(4):
                                        if sst[f"scfree{i}"] is not None:
                                            w = sst[f"scfree{i}"]
                                            t.wait_ge(w[0], w[1])
                                            sst[f"scfree{i}"] = None
                                        if i == 2:
                                            t.wait_ge(b.s_pe.h, qktok[1])
                                        t.ldweights(
                                            kT[32 * i:32 * i + 32, g, 128 * s_i:128 * s_i + 128],
                                            tile_position=(32 * i, 0))
                                        qktok[i] = b.s_pe.inc(t.matmul(
                                            psSc[i][:],
                                            kT[32 * i:32 * i + 32, g, 128 * s_i:128 * s_i + 128],
                                            qT[32 * i:32 * i + 32, g, 512 * tau:512 * tau + 512],
                                            start=True, stop=True,
                                            tile_position=(32 * i, 0)))
                                block.tensor(tq)
                                if ATTN_SUB < 2:
                                    continue

                                job = {"g": g, "s": s_i, "par": par, "exp": {},
                                       "avtok": {}}

                                def ta(sc, g=g, s_i=s_i, par=par, qktok=qktok,
                                       job=job):
                                    for i in range(4):
                                        sc.wait_ge(b.s_pe.h, qktok[i])
                                        if sst[f"av{i}{par}"]:
                                            sc.wait_ge(b.s_pe.h, sst[f"av{i}{par}"])
                                        sst[f"exp{i}"] = b.s_act.inc(sc.activation(
                                            e_sb[:, i, par, :], psSc[i][:], AF.Exp,
                                            scale=SCAL))
                                        job["exp"][i] = sst[f"exp{i}"]
                                block.scalar(ta)
                                if ATTN_SUB < 3:
                                    continue

                                if avpend[0] is not None:
                                    do_avjob(avpend[0])
                                avpend[0] = job
                            if avpend[0] is not None:
                                do_avjob(avpend[0])
                                avtok = avpend[0]["avtok"]
                                avpend[0] = None

                            for jj in range(2 if ATTN_STAGE >= 6 else 0):
                                A0 = psA[2 * jj]
                                A1 = psA[2 * jj + 1]
                                jg = 2 * g + jj
                                ep = {}
                                rs0 = rowscr[64:65, 0:512]
                                rs1 = rowscr[64:65, 512:1024]
                                rb0 = rowscr[0:1, 0:512]
                                rb1 = rowscr[0:1, 512:1024]

                                def ev1(v, A0=A0, A1=A1, ep=ep):
                                    v.wait_ge(b.s_pe.h, ep["avmax"])
                                    if ep.get("rowscr_pe"):
                                        v.wait_ge(b.s_pe.h, ep["rowscr_pe"])
                                    guard_v(v, v.tensor_copy(rs0, A0[64:65, :]))
                                    guard_v(v, v.tensor_copy(rs1, A1[64:65, :]))
                                    guard_v(v, v.reciprocal(rs1, rs1))
                                    guard_v(v, v.tensor_mul(rs1, rs0, rs1))
                                    ep["crow"] = guard_v(v, v.tensor_scalar(
                                        rs1, rs1, lamcol[64:65, 0:1], None, ALU.mult))
                                ep["avmax"] = max(avtok.values())
                                ep["rowscr_pe"] = sst.get("row_pe", 0)
                                block.vector(ev1)

                                def et1(t, ep=ep, jj=jj):
                                    t.wait_ge(b.s_dve.h, ep["crow"])
                                    ep["mmC"] = b.s_pe.inc(t.matmul(
                                        psSc[0][0:64, :], ones[64:65, 0:64], rs1,
                                        start=True, stop=True, tile_position=(64, 0)))
                                    ep["mmE"] = b.s_pe.inc(t.matmul(
                                        psSc[1][0:1, :], ones[64:65, 0:1], rs0,
                                        start=True, stop=True, tile_position=(64, 0)))
                                block.tensor(et1)
                                sst["row_pe"] = ep["mmE"]

                                def ev2(v, A0=A0, A1=A1, ep=ep):
                                    v.wait_ge(b.s_pe.h, ep["mmC"])
                                    ep["c6"] = guard_v(v, v.tensor_copy(scr_cr[0:64, :],
                                                                        psSc[0][0:64, :]))
                                    v.tensor_mul(scr_t[0:64, :], A1[0:64, :], scr_cr[0:64, :])
                                    ep["o"] = guard_v(v, v.tensor_sub(
                                        scr_o[0:64, :], A0[0:64, :], scr_t[0:64, :]))
                                    ep["sq"] = guard_v(v, v.tensor_mul(
                                        scr_t[0:64, :], scr_o[0:64, :], scr_o[0:64, :]))
                                    v.wait_ge(b.s_pe.h, ep["mmE"])
                                    ep["er"] = guard_v(v, v.tensor_copy(rb0, psSc[1][0:1, :]))
                                    guard_v(v, v.tensor_mul(rb1, rb0, rb0))
                                    ep["ersq"] = guard_v(v, v.tensor_single_scalar(
                                        rb1, rb1, 1e-5, ALU.mult))
                                block.vector(ev2)

                                def et2(t, ep=ep):
                                    t.wait_ge(b.s_dve.h, ep["sq"])
                                    ep["mmS"] = b.s_pe.inc(t.matmul(
                                        psSc[2][0:1, :], ones[0:64, 0:1], scr_t[0:64, :],
                                        start=True, stop=True))
                                block.tensor(et2)

                                def ev3(v, ep=ep):
                                    v.wait_ge(b.s_pe.h, ep["mmS"])
                                    ep["u2"] = guard_v(v, v.scalar_tensor_tensor(
                                        rb0, psSc[2][0:1, :], 1.0 / 64.0, rb1,
                                        ALU.mult, ALU.add))
                                block.vector(ev3)

                                def ea1(sc, ep=ep):
                                    sc.wait_ge(b.s_dve.h, ep["u2"])
                                    guard_a(sc, sc.activation(rb1, rb0, AF.Ln))
                                    ep["rstd"] = guard_a(sc, sc.activation(
                                        rb1, rb1, AF.Exp, scale=-0.5))
                                block.scalar(ea1)

                                def et3(t, ep=ep):
                                    t.wait_ge(b.s_act.h, ep["rstd"])
                                    ep["mmR"] = b.s_pe.inc(t.matmul(
                                        psSc[3][0:64, :], ones[0:1, 0:64], rb1,
                                        start=True, stop=True))
                                block.tensor(et3)
                                sst["row_pe"] = ep["mmR"]

                                def ev4(v, ep=ep, jg=jg):
                                    v.wait_ge(b.s_pe.h, ep["mmR"])
                                    ep["c14"] = guard_v(v, v.tensor_copy(
                                        scr_cr[0:64, :], psSc[3][0:64, :]))
                                    v.tensor_mul(scr_t[0:64, :], scr_o[0:64, :],
                                                 scr_cr[0:64, :])
                                    if sst["wo_pe"]:
                                        v.wait_ge(b.s_pe.h, sst["wo_pe"])
                                    ep["nrm"] = b.s_dve.inc(v.tensor_scalar(
                                        nrm[0:64, jg, :], scr_t[0:64, :],
                                        sublncol[0:64, 0:1], None, ALU.mult))
                                block.vector(ev4)
                                sst["nrm_dve"] = ep["nrm"]
                                if DEBUG_DUMP and "epi" in DEBUG_SEL and tau == 0 and g == 0 and jj == 0:
                                    def dmp(g_, ep=ep):
                                        g_.wait_ge(b.s_dve.h, ep["nrm"])
                                        b.s_sp.inc(g_.dma_start(out=dbg["lamcol"], in_=lamcol[:]), 16)
                                        b.s_sp.inc(g_.dma_start(out=dbg["rowscr"], in_=rowscr[:]), 16)
                                        b.s_sp.inc(g_.dma_start(out=dbg["scro"], in_=scr_o[:]), 16)
                                        b.s_sp.inc(g_.dma_start(out=dbg["nrm0"], in_=nrm[:, 0, :]), 16)
                                        b.s_sp.inc(g_.dma_start(out=dbg["e0"], in_=e_sb[:, 0, 1, :]), 16)
                                    block.gpsimd(dmp)

                                sst[f"pafree{2 * jj}"] = (b.s_dve.h, ep["o"])
                                sst[f"pafree{2 * jj + 1}"] = (b.s_dve.h, ep["o"])
                                sst["scfree0"] = (b.s_dve.h, ep["c6"])
                                sst["scfree1"] = (b.s_dve.h, ep["er"])
                                sst["scfree2"] = (b.s_dve.h, ep["u2"])
                                sst["scfree3"] = (b.s_dve.h, ep["c14"])

                        if DEBUG_DUMP and "nrm" in DEBUG_SEL and tau == 0:
                            def dmpn(g_):
                                g_.wait_ge(b.s_dve.h, sst["nrm_dve"])
                                b.s_sp.inc(g_.dma_start(out=dbg["nrmall"], in_=nrm[:]), 16)
                            block.gpsimd(dmpn)
                        for fo in range(2 if ATTN_STAGE >= 7 else 0):
                            for cc in range(8):
                                par = cc % 4
                                wtok = {}

                                castsem = b.s_cast if par % 2 == 0 else b.s_cast1

                                def gsec(g_, fo=fo, cc=cc, par=par, wtok=wtok,
                                         castsem=castsem):
                                    if sst.get(f"wob_pe{par}"):
                                        g_.wait_ge(b.s_pe.h, sst[f"wob_pe{par}"])
                                    castsem.inc(g_.dma_start(
                                        out=wob[0:64, par, :],
                                        in_=wo_d[128 * cc:128 * cc + 64,
                                                 512 * fo:512 * fo + 512]), 16)
                                    wtok["ld"] = castsem.inc(g_.dma_start(
                                        out=wob2[0:64, par, :],
                                        in_=wo_d[128 * cc + 64:128 * cc + 128,
                                                 512 * fo:512 * fo + 512]), 16)
                                block.gpsimd(gsec)

                                def tsec(t, fo=fo, cc=cc, par=par, wtok=wtok, tau=tau,
                                         castsem=castsem):
                                    t.wait_ge(castsem.h, wtok["ld"])
                                    if cc == 0:
                                        t.wait_ge(b.s_dve.h, sst["nrm_dve"])
                                        for tt in range(4):
                                            w = sst[f"pafree{tt}"]
                                            if w is not None:
                                                t.wait_ge(w[0], w[1])
                                                sst[f"pafree{tt}"] = None
                                            if sst.get(f"wores{tt}"):
                                                t.wait_ge(b.s_dve.h, sst[f"wores{tt}"])
                                    im = None
                                    for tt in range(4):
                                        im = t.matmul(
                                            psA[tt][:],
                                            nrm[0:64, 2 * cc, 128 * tt:128 * tt + 128],
                                            wob[0:64, par, :],
                                            start=(cc == 0), stop=False)
                                        last = (cc == 7)
                                        im = t.matmul(
                                            psA[tt][:],
                                            nrm[0:64, 2 * cc + 1, 128 * tt:128 * tt + 128],
                                            wob2[0:64, par, :],
                                            start=False, stop=last)
                                        wtok[f"m{tt}"] = b.s_pe.inc(im)
                                    sst[f"wob_pe{par}"] = wtok["m3"]
                                block.tensor(tsec)

                                if cc == 7:
                                    def vres(v, fo=fo, tau=tau, wtok=wtok):
                                        for tt in range(4):
                                            v.wait_ge(b.s_pe.h, wtok[f"m{tt}"])
                                            gt = 4 * tau + tt
                                            sst[f"wores{tt}"] = b.s_dve.inc(v.tensor_add(
                                                xres[:, gt, 512 * fo:512 * fo + 512],
                                                xres[:, gt, 512 * fo:512 * fo + 512],
                                                psA[tt][:]))
                                    block.vector(vres)
                        sst["wo_pe"] = b.s_pe.n
                        for tt in range(4):
                            if sst.get(f"wores{tt}"):
                                sst[f"pafree{tt}"] = (b.s_dve.h, sst[f"wores{tt}"])

                    barrier()

        ffn(first=True)
        if n_phases >= 2:
            attention()
        if n_phases >= 3:
            ffn(first=False)

        barrier()

        def store_out(s):
            for t in range(TC):
                b.s_st.inc(s.dma_start(out=out_d[128 * t:128 * (t + 1), :],
                                       in_=xres[:, t, :]), 16)
            s.wait_ge(b.s_st.h, b.s_st.n)
        block.sync(store_out)

    return nc


def kernel(L0_attn_query, L0_attn_Wq, L0_attn_Wk, L0_attn_Wv, L0_attn_Wo,
           L0_attn_lambda_q1, L0_attn_lambda_k1, L0_attn_lambda_q2,
           L0_attn_lambda_k2, L0_attn_subln_w, L0_ffn_w_gate, L0_ffn_w_up,
           L0_ffn_w_down):
    x = np.ascontiguousarray(np.asarray(L0_attn_query, dtype=np.float32))
    nc = build_program(N_PHASES)
    lam4 = np.concatenate([np.asarray(L0_attn_lambda_q1, np.float32).ravel(),
                           np.asarray(L0_attn_lambda_k1, np.float32).ravel(),
                           np.asarray(L0_attn_lambda_q2, np.float32).ravel(),
                           np.asarray(L0_attn_lambda_k2, np.float32).ravel()])
    common = {
        "wq": np.ascontiguousarray(np.asarray(L0_attn_Wq, np.float32).T),
        "wk": np.ascontiguousarray(np.asarray(L0_attn_Wk, np.float32).T),
        "wv": np.ascontiguousarray(np.asarray(L0_attn_Wv, np.float32).T),
        "wo": np.ascontiguousarray(np.asarray(L0_attn_Wo, np.float32).T),
        "wg": np.ascontiguousarray(np.asarray(L0_ffn_w_gate, np.float32)),
        "wu": np.ascontiguousarray(np.asarray(L0_ffn_w_up, np.float32)),
        "wd": np.ascontiguousarray(np.asarray(L0_ffn_w_down, np.float32)),
        "lam4": np.ascontiguousarray(lam4.reshape(1, 128)),
        "subln": np.asarray(L0_attn_subln_w, np.float32).reshape(64, 1).copy(),
        "idn": np.eye(128, dtype=BF),
        "onesrow": np.ones((128, 128), np.float32),
    }
    in_maps = [dict(common, x=np.ascontiguousarray(x[c])) for c in range(8)]
    res = run_bass_kernel_spmd(nc, in_maps, core_ids=list(range(8)))
    out = np.stack([np.asarray(res.results[c]["out"]) for c in range(8)], axis=0)
    return out.astype(np.float32)


# version 10 (final)
# speedup vs baseline: 1.0807x; 1.0498x over previous
import math
import numpy as np
import ml_dtypes
from contextlib import ExitStack

import concourse.bass as bass
import concourse.mybir as mybir
from concourse.bass_utils import run_bass_kernel_spmd

F32 = mybir.dt.float32
BF16 = mybir.dt.bfloat16
I32 = mybir.dt.int32
I16 = mybir.dt.int16
AF = mybir.ActivationFunctionType
ALU = mybir.AluOpType
AX = mybir.AxisListType
BF = ml_dtypes.bfloat16

T = 2048
TC = 16
D = 1024
DC = 8
HID = 2816
HC = 22
STRIPS = [(0, 512), (512, 512), (1024, 512), (1536, 512), (2048, 512), (2560, 256)]
NSUP = 4
LAM0 = float(0.8 - 0.6 * math.exp(-0.3 * 12))
SCAL = 1.0 / math.sqrt(32.0)

N_PHASES = 3
DEBUG_DUMP = False
DEBUG_SEL = "ffn,proj,epi,nrm"
ATTN_STAGE = 9
ATTN_NTAU = 4
ATTN_NG = 8
ATTN_NS = 16
ATTN_SUB = 3


class Sem:
    def __init__(self, ctx, nc, name):
        self.h = ctx.enter_context(nc.semaphore(name))
        self.n = 0

    def inc(self, inst, k=1):
        inst.then_inc(self.h, k)
        self.n += k
        return self.n


def build_program(n_phases=3):
    nc = bass.Bass("TRN2", target_bir_lowering=False)

    x_d = nc.dram_tensor("x", [T, D], F32, kind="ExternalInput").ap()
    wq_d = nc.dram_tensor("wq", [D, D], F32, kind="ExternalInput").ap()
    wk_d = nc.dram_tensor("wk", [D, D], F32, kind="ExternalInput").ap()
    wv_d = nc.dram_tensor("wv", [D, D], F32, kind="ExternalInput").ap()
    wo_d = nc.dram_tensor("wo", [D, D], F32, kind="ExternalInput").ap()
    wg_d = nc.dram_tensor("wg", [HID, D], F32, kind="ExternalInput").ap()
    wu_d = nc.dram_tensor("wu", [HID, D], F32, kind="ExternalInput").ap()
    wd_d = nc.dram_tensor("wd", [D, HID], F32, kind="ExternalInput").ap()
    lam_d = nc.dram_tensor("lam4", [1, 128], F32, kind="ExternalInput").ap()
    subln_d = nc.dram_tensor("subln", [64, 1], F32, kind="ExternalInput").ap()
    idn_d = nc.dram_tensor("idn", [128, 128], BF16, kind="ExternalInput").ap()
    ones_d = nc.dram_tensor("onesrow", [128, 128], F32, kind="ExternalInput").ap()
    out_d = nc.dram_tensor("out", [T, D], F32, kind="ExternalOutput").ap()
    dbg = {}
    if DEBUG_DUMP:
        dbg["ln"] = nc.dram_tensor("dbg_ln", [128, D], F32, kind="ExternalOutput").ap()
        dbg["xq"] = nc.dram_tensor("dbg_xq", [128, D], BF16, kind="ExternalOutput").ap()
        dbg["gate"] = nc.dram_tensor("dbg_gate", [128, 512], F32, kind="ExternalOutput").ap()
        dbg["up"] = nc.dram_tensor("dbg_up", [128, 512], F32, kind="ExternalOutput").ap()
        dbg["h"] = nc.dram_tensor("dbg_h", [128, 512], F32, kind="ExternalOutput").ap()
        dbg["hq"] = nc.dram_tensor("dbg_hq", [128, 512], BF16, kind="ExternalOutput").ap()
        dbg["wdq0"] = nc.dram_tensor("dbg_wdq0", [128, D], BF16, kind="ExternalOutput").ap()
        dbg["lamcol"] = nc.dram_tensor("dbg_lamcol", [128, 1], F32, kind="ExternalOutput").ap()
        dbg["rowscr"] = nc.dram_tensor("dbg_rowscr", [128, 1024], F32, kind="ExternalOutput").ap()
        dbg["scro"] = nc.dram_tensor("dbg_scro", [128, 512], F32, kind="ExternalOutput").ap()
        dbg["nrm0"] = nc.dram_tensor("dbg_nrm0", [128, 512], BF16, kind="ExternalOutput").ap()
        dbg["e0"] = nc.dram_tensor("dbg_e0", [128, 512], BF16, kind="ExternalOutput").ap()
        dbg["qt0"] = nc.dram_tensor("dbg_qt0", [128, 512], BF16, kind="ExternalOutput").ap()
        dbg["kt0"] = nc.dram_tensor("dbg_kt0", [128, 2048], BF16, kind="ExternalOutput").ap()
        dbg["v0"] = nc.dram_tensor("dbg_v0", [128, 1040], BF16, kind="ExternalOutput").ap()
        dbg["nrmall"] = nc.dram_tensor("dbg_nrmall", [128, 16, 512], BF16, kind="ExternalOutput").ap()
        dbg["ln2t0"] = nc.dram_tensor("dbg_ln2t0", [128, 512], BF16, kind="ExternalOutput").ap()
    wgq_s = nc.dram_tensor("wgq_s", [DC, 128, HID], BF16, kind="Internal").ap()
    wuq_s = nc.dram_tensor("wuq_s", [DC, 128, HID], BF16, kind="Internal").ap()
    wdq_s = nc.dram_tensor("wdq_s", [HC, 128, D], BF16, kind="Internal").ap()

    ctx = ExitStack()
    with ctx:
        ent = ctx.enter_context
        xres = ent(nc.sbuf_tensor("xres", [128, TC, D], F32))
        idn = ent(nc.sbuf_tensor("idn_sb", [128, 128], BF16))
        ones = ent(nc.sbuf_tensor("ones_sb", [128, 128], F32))
        lamcol = ent(nc.sbuf_tensor("lamcol", [128, 1], F32))
        sublncol = ent(nc.sbuf_tensor("sublncol", [128, 1], F32))
        stats = ent(nc.sbuf_tensor("stats", [128, 18], F32))
        maskp = ent(nc.sbuf_tensor("maskp", [128, 1], I32))
        masks = ent(nc.sbuf_tensor("masks", [128, 1], I32))
        two16 = ent(nc.sbuf_tensor("two16", [128, 1], I16))
        eps5col = ent(nc.sbuf_tensor("eps5col", [128, 1], F32))
        bigscr = ent(nc.sbuf_tensor("bigscr", [128, 1024], F32))
        rowscr = ent(nc.sbuf_tensor("rowscr", [128, 1024], F32))
        Q = {}

        block = ent(nc.Block())

        class BB:
            pass
        b = BB()
        for name in ["s_ld", "s_cast", "s_cast1", "s_sp", "s_pe", "s_act", "s_dve",
                     "s_gp", "s_st", "s_bar", "s_lw", "s_lst", "s_lwd"]:
            setattr(b, name, Sem(ctx, nc, name))

        st = {"qdve": 0, "qgp": 0, "qact_t2": 0, "wld": 0, "pend": None,
              "slot": 0, "wd_ev": 0, "gu_ev": 0}

        def barrier():
            toks = [(b.s_pe.h, b.s_pe.n), (b.s_act.h, b.s_act.n),
                    (b.s_dve.h, b.s_dve.n), (b.s_gp.h, b.s_gp.n)]
            base = b.s_bar.n
            for sec in [block.sync, block.tensor, block.vector, block.scalar,
                        block.gpsimd]:
                def _f(eng):
                    for h, n in toks:
                        if n:
                            eng.wait_ge(h, n)
                    b.s_bar.inc(eng.nop(), 1)
                sec(_f)
            target = base + 5
            for sec in [block.sync, block.tensor, block.vector, block.scalar,
                        block.gpsimd]:
                sec(lambda eng: eng.wait_ge(b.s_bar.h, target))

        def ph0_sync(s):
            for t in range(TC):
                b.s_ld.inc(s.dma_start(out=xres[:, t, :],
                                       in_=x_d[128 * t:128 * (t + 1), :]), 16)
            b.s_ld.inc(s.dma_start(out=idn[:], in_=idn_d[:]), 16)
            b.s_ld.inc(s.dma_start(out=ones[:], in_=ones_d[:]), 16)
            b.s_ld.inc(s.dma_start(out=rowscr[0:1, 0:128], in_=lam_d[:, :]), 16)
            b.s_ld.inc(s.dma_start(out=sublncol[0:64, :], in_=subln_d[:, :]), 16)
        block.sync(ph0_sync)
        tok_ph0 = b.s_ld.n

        def ph0_vec(v):
            v.wait_ge(b.s_ld.h, tok_ph0)
            v.memset(maskp[:], 0x7FFFFFFF)
            v.memset(masks[:], -2147483648)
            t = b.s_dve.inc(v.memset(two16[:], 2))
            v.wait_ge(b.s_dve.h, t)
            t = b.s_dve.inc(v.memset(eps5col[:], 1e-5))
            v.wait_ge(b.s_dve.h, t)
            t = b.s_dve.inc(v.tensor_single_scalar(sublncol[0:64, :], sublncol[0:64, :],
                                                   1.0 - LAM0, ALU.mult))
            v.wait_ge(b.s_dve.h, t)
        def ph0_gate(s):
            s.wait_ge(b.s_ld.h, tok_ph0)
        block.sync(ph0_gate)
        barrier()
        block.vector(ph0_vec)

        def flush_q():
            pe = st.get("pend")
            if pe is None:
                return st["qdve"]
            st["pend"] = None
            p, F, src, dst = pe["p"], pe["F"], pe["src"], pe["dst"]
            BL = F // 16
            shq = Q["sh2"][p] if pe["halve"] else Q["sh"][p]
            tok = {}

            def vsec2(v, pe=pe, p=p, F=F, BL=BL, shq=shq, tok=tok, src=src, dst=dst):
                v.wait_ge(b.s_gp.h, pe["gp"])
                dw = pe["dst_war"]() if callable(pe["dst_war"]) else pe["dst_war"]
                if dw is not None:
                    v.wait_ge(dw[0], dw[1])
                xn2 = Q["xn2"][p][:, :F]
                t2 = Q["t2s"][:, :F]
                b1h = Q["b1h"][:, :F]
                v.tensor_single_scalar(t2.bitcast(I32), xn2.bitcast(I32),
                                       maskp[:], ALU.bitwise_and)
                v.tensor_single_scalar(Q["A"][:, :F], t2, 4.0, ALU.min)
                v.tensor_scalar(b1h, t2, 0.5, -2.0, ALU.mult, ALU.add)
                v.tensor_scalar(Q["B"][:, :F], b1h, 0.0, 2.0, ALU.max, ALU.min)
                v.tensor_scalar(b1h, t2, -9.5, 0.0, ALU.add, ALU.max)
                v.tensor_single_scalar(Q["C"][:, :F], b1h, 1.0, ALU.min)
                v.scalar_tensor_tensor(Q["C"][:, :F], Q["C"][:, :F], two16[:],
                                       Q["B"][:, :F], ALU.mult, ALU.add)
                v.scalar_tensor_tensor(Q["B"][:, :F], Q["C"][:, :F], two16[:],
                                       Q["A"][:, :F], ALU.mult, ALU.add)
                shb = shq[:, :BL].unsqueeze(2).broadcast_to([128, BL, 16])
                v.scalar_tensor_tensor(
                    t2.bitcast(I32).rearrange("p (bb k) -> p bb k", k=16),
                    xn2.bitcast(I32).rearrange("p (bb k) -> p bb k", k=16),
                    masks[:], shb.bitcast(I32), ALU.bitwise_and, ALU.bitwise_or)
                tok["q"] = b.s_dve.inc(v.tensor_tensor(dst, Q["B"][:, :F], t2,
                                                       ALU.mult))
            block.vector(vsec2)
            st["qdve"] = tok["q"]
            if pe["post"] is not None:
                pe["post"](tok["q"])
            return tok["q"]

        def emit_quant(src, dst, F, halve=False, dst_war=None, src_ready=None,
                       post=None):
            BL = F // 16
            p = st["slot"] & 1
            st["slot"] += 1
            tok = {}

            def vsec1(v, p=p):
                if src_ready is not None:
                    v.wait_ge(src_ready[0], src_ready[1])
                sv = src.rearrange("p (bb k) -> p bb k", k=16)
                t = b.s_dve.inc(v.tensor_reduce(
                    Q["m"][:, :BL], sv, AX.X, ALU.max, apply_absolute_value=True))
                v.wait_ge(b.s_dve.h, t)
                t = b.s_dve.inc(v.tensor_scalar(
                    Q["sh"][p][:, :BL], Q["m"][:, :BL], 1.0 / 12.0, 5e-13,
                    ALU.mult, ALU.max))
                v.wait_ge(b.s_dve.h, t)
                tok["r2"] = b.s_dve.inc(v.reciprocal(Q["r2"][p][:, :BL],
                                                     Q["sh"][p][:, :BL]))
                if halve:
                    tok["r2"] = b.s_dve.inc(v.tensor_single_scalar(
                        Q["sh2"][p][:, :BL], Q["sh"][p][:, :BL], 0.5, ALU.mult))
                v.wait_ge(b.s_dve.h, tok["r2"])
            block.vector(vsec1)

            def gsec(g, p=p):
                g.wait_ge(b.s_dve.h, tok["r2"])
                if src_ready is not None:
                    g.wait_ge(src_ready[0], src_ready[1])
                if st["qdve"]:
                    g.wait_ge(b.s_dve.h, st["qdve"])
                r2b = Q["r2"][p][:, :BL].unsqueeze(2).broadcast_to([128, BL, 16])
                xn = Q["xn2"][p]
                tok["gp"] = b.s_gp.inc(g.tensor_tensor(
                    xn[:, :F].rearrange("p (bb k) -> p bb k", k=16),
                    src.rearrange("p (bb k) -> p bb k", k=16), r2b, ALU.mult))
            block.gpsimd(gsec)
            st["qgp"] = tok["gp"]
            st["qr2"] = tok["r2"]

            flush_q()
            st["pend"] = dict(p=p, F=F, src=src, dst=dst, halve=halve,
                              dst_war=dst_war, post=post, gp=tok["gp"])
            return tok

        def emit_ln(src, dst, dst_war=None):
            tok = {}

            def vsec(v):
                t = b.s_dve.inc(v.bn_stats(stats[:, 0:6], src[:, 0:512]))
                t = b.s_dve.inc(v.bn_stats(stats[:, 8:14], src[:, 512:1024]))
                v.wait_ge(b.s_dve.h, t)
                t = b.s_dve.inc(v.tensor_copy(stats[:, 6:8], stats[:, 8:10]))
                v.wait_ge(b.s_dve.h, t)
                t = b.s_dve.inc(v.tensor_copy(stats[:, 8:12], stats[:, 10:14]))
                v.wait_ge(b.s_dve.h, t)
                tok["mv"] = b.s_dve.inc(v.bn_aggr(stats[:, 14:16], stats[:, 0:12]))
                v.wait_ge(b.s_dve.h, tok["mv"])
            block.vector(vsec)

            def asec2(sc):
                sc.wait_ge(b.s_dve.h, tok["mv"])
                t = b.s_act.inc(sc.activation(stats[:, 16:17], stats[:, 15:16], AF.Ln,
                                              bias=eps5col[:, 0:1]))
                sc.wait_ge(b.s_act.h, t)
                tok["rstd"] = b.s_act.inc(sc.activation(stats[:, 16:17], stats[:, 16:17],
                                                        AF.Exp, scale=-0.5))
            block.scalar(asec2)

            def vsec2(v):
                v.wait_ge(b.s_act.h, tok["rstd"])
                t = b.s_dve.inc(v.tensor_scalar(stats[:, 17:18], stats[:, 14:15],
                                                stats[:, 16:17], -1.0,
                                                ALU.mult, ALU.mult))
                v.wait_ge(b.s_dve.h, t)
                if st["qgp"]:
                    v.wait_ge(b.s_gp.h, st["qgp"])
                if dst_war is not None:
                    v.wait_ge(dst_war[0], dst_war[1])
                tok["out"] = b.s_dve.inc(v.tensor_scalar(dst, src, stats[:, 16:17],
                                                         stats[:, 17:18],
                                                         ALU.mult, ALU.add))
            block.vector(vsec2)
            return tok["out"]

        def ffn(first):
            sfx = "a" if first else "b"
            fctx = ExitStack()
            with fctx:
                fent = fctx.enter_context
                wld = fent(nc.sbuf_tensor("wld" + sfx, [128, HID], F32))
                Q["xn2"] = [fent(nc.sbuf_tensor(f"qs_xn2{i}" + sfx, [128, 1024], F32))
                            for i in range(2)]
                Q["t2s"] = bigscr
                Q["b1h"] = fent(nc.sbuf_tensor("qs_b1h" + sfx, [128, 1024], F32))
                Q["A"] = fent(nc.sbuf_tensor("qs_A" + sfx, [128, 1024], I16))
                Q["B"] = fent(nc.sbuf_tensor("qs_B" + sfx, [128, 1024], I16))
                Q["C"] = fent(nc.sbuf_tensor("qs_C" + sfx, [128, 1024], I16))
                Q["m"] = fent(nc.sbuf_tensor("qs_m" + sfx, [128, 64], F32))
                Q["sh"] = [fent(nc.sbuf_tensor(f"qs_sh{i}" + sfx, [128, 64], F32))
                           for i in range(2)]
                Q["sh2"] = [fent(nc.sbuf_tensor(f"qs_sh2{i}" + sfx, [128, 64], F32))
                            for i in range(2)]
                Q["r2"] = [fent(nc.sbuf_tensor(f"qs_r2{i}" + sfx, [128, 64], F32))
                           for i in range(2)]
                ln_sb = wld[:, 0:D]
                wq_chunk = fent(nc.sbuf_tensor("wq_chunk" + sfx, [128, 1024], BF16))
                wgstrip = fent(nc.sbuf_tensor("wgstrip" + sfx, [128, DC, 512], BF16))
                wustrip = fent(nc.sbuf_tensor("wustrip" + sfx, [128, DC, 512], BF16))
                wdqT = fent(nc.sbuf_tensor("wdqT" + sfx, [128, HC, D], BF16))
                xqT = fent(nc.sbuf_tensor("xqT" + sfx, [128, DC, 512], BF16))
                hqT = fent(nc.sbuf_tensor("hqT" + sfx, [128, HC, 512], BF16))
                xq_sb = fent(nc.sbuf_tensor("xq_sb" + sfx, [128, D], BF16))
                silu_sb = [fent(nc.sbuf_tensor(f"silu{i}" + sfx, [128, 512], F32)) for i in range(2)]
                h_sb = [fent(nc.sbuf_tensor(f"h{i}" + sfx, [128, 512], F32)) for i in range(2)]
                hq_sb = [fent(nc.sbuf_tensor(f"hq{i}" + sfx, [128, 512], BF16)) for i in range(2)]
                psG = [fent(nc.psum_tensor(f"psG{i}" + sfx, [128, 512], F32)) for i in range(2)]
                psU = [fent(nc.psum_tensor(f"psU{i}" + sfx, [128, 512], F32)) for i in range(2)]
                psD = [fent(nc.psum_tensor(f"psD{i}" + sfx, [128, 512], F32)) for i in range(2)]
                psT = [fent(nc.psum_tensor(f"psT{i}" + sfx, [128, 128], BF16)) for i in range(2)]

                fst = {"tp": 0, "evac0": 0, "evac1": 0, "strip_pe": 0,
                       "sup_pe": 0, "silu_rd0": 0, "silu_rd1": 0,
                       "hpipe0": 0, "hpipe1": 0, "dres0": 0, "dres1": 0,
                       "wq_pe": 0}

                def emit_tp(src, dst, pe_wait=None, evac_war=()):
                    i = fst["tp"] % 2
                    fst["tp"] += 1
                    tk = {}
                    wars = [w for w in (evac_war if isinstance(evac_war, (list, tuple)) and
                                        (not evac_war or isinstance(evac_war[0], (list, tuple)))
                                        else [evac_war]) if w]

                    def tsec(t):
                        if pe_wait is not None:
                            t.wait_ge(pe_wait[0], pe_wait[1])
                        if fst[f"evac{i}"]:
                            t.wait_ge(b.s_dve.h, fst[f"evac{i}"])
                        tk["tp"] = b.s_pe.inc(t.transpose(psT[i][:], src, idn[:]))
                    block.tensor(tsec)

                    def vsec(v):
                        v.wait_ge(b.s_pe.h, tk["tp"])
                        for w in wars:
                            v.wait_ge(w[0], w[1])
                        tk["ev"] = b.s_dve.inc(v.tensor_copy(dst, psT[i][:]))
                    block.vector(vsec)
                    fst[f"evac{i}"] = tk["ev"]
                    return tk["ev"]

                def wq_war():
                    return (b.s_pe.h, fst["wq_pe"]) if fst["wq_pe"] else None

                def quant_w_chunk(dram_rows, nf, dst, halve, post=None):
                    def ssec(s):
                        if st["qgp"]:
                            s.wait_ge(b.s_gp.h, st["qgp"])
                        if st.get("qr2"):
                            s.wait_ge(b.s_dve.h, st["qr2"])
                        st["wld"] = b.s_lw.inc(
                            s.dma_start(out=wld[:, :nf], in_=dram_rows), 16)
                    block.sync(ssec)

                    def vw(v):
                        v.wait_ge(b.s_lw.h, st["wld"])
                    block.vector(vw)
                    off = 0
                    while off < nf:
                        piece = min(1024, nf - off)
                        qret = emit_quant(wld[:, off:off + piece],
                                          dst[:, off:off + piece], piece,
                                          halve=halve, dst_war=wq_war,
                                          post=post(off, piece) if post else None)
                        st["w_gp"] = qret["gp"]
                        st["w_r2"] = qret["r2"]
                        off += piece

                if first:
                    for c in range(DC):
                        def ssec(s, c=c):
                            if st["qgp"]:
                                s.wait_ge(b.s_gp.h, st["qgp"])
                            if st.get("qr2"):
                                s.wait_ge(b.s_dve.h, st["qr2"])
                            st["wld"] = b.s_lw.inc(s.dma_start(
                                out=wld[:, :HID],
                                in_=wd_d[128 * c:128 * (c + 1), :]), 16)
                        block.sync(ssec)

                        def vw(v):
                            v.wait_ge(b.s_lw.h, st["wld"])
                        block.vector(vw)
                        for off, piece in ((0, 1024), (1024, 1024), (2048, 768)):
                            def mkpost(off=off, piece=piece, c=c):
                                def post(qtok):
                                    for p2 in range(piece // 128):
                                        cc = (off + 128 * p2) // 128
                                        st["wd_ev"] = emit_tp(
                                            wq_chunk[:, 128 * p2:128 * (p2 + 1)],
                                            wdqT[:, cc, 128 * c:128 * (c + 1)],
                                            pe_wait=(b.s_dve.h, qtok))
                                    fst["wq_pe"] = b.s_pe.n
                                return post
                            qret = emit_quant(wld[:, off:off + piece],
                                              wq_chunk[:, :piece], piece,
                                              halve=True, dst_war=wq_war,
                                              post=mkpost())
                            st["w_gp"] = qret["gp"]
                            st["w_r2"] = qret["r2"]
                    flush_q()

                    def spill_wd(g, ev=st["wd_ev"]):
                        g.wait_ge(b.s_dve.h, ev)
                        if DEBUG_DUMP and "ffn" in DEBUG_SEL:
                            b.s_sp.inc(g.dma_start(out=dbg["wdq0"], in_=wdqT[:, 0, :]), 16)
                        for cc in range(HC):
                            b.s_sp.inc(g.dma_start(out=wdq_s[cc, :, :],
                                                   in_=wdqT[:, cc, :]), 16)
                    block.gpsimd(spill_wd)
                else:
                    def reload_wd(s):
                        s.wait_ge(b.s_sp.h, b.s_sp.n)
                        for cc in range(HC):
                            b.s_lwd.inc(s.dma_start(out=wdqT[:, cc, :],
                                                    in_=wdq_s[cc, :, :]), 16)
                        st["wd_re"] = b.s_lwd.n
                    block.sync(reload_wd)

                    def wait_wd(t):
                        t.wait_ge(b.s_lwd.h, st["wd_re"])
                    block.tensor(wait_wd)

                ln_slots = [wld[:, 0:D], wld[:, 1024:1024 + D]]
                for S in range(NSUP):
                    tts = [4 * S + i for i in range(4)]
                    for k, tt in enumerate(tts):
                        ln_dst = ln_slots[k & 1]
                        lntok = emit_ln(xres[:, tt, :], ln_dst)
                        if DEBUG_DUMP and "ffn" in DEBUG_SEL and first and tt == 2:
                            def dmp(g, lntok=lntok, ln_dst=ln_dst):
                                g.wait_ge(b.s_dve.h, lntok)
                                b.s_sp.inc(g.dma_start(out=dbg["ln"], in_=ln_dst), 16)
                            block.gpsimd(dmp)

                        def mkpost(k=k, tt=tt):
                            def post(qtok):
                                if (DEBUG_DUMP and "ffn" in DEBUG_SEL and first
                                        and tt == 2):
                                    def dmp2(g, qtok=qtok):
                                        g.wait_ge(b.s_dve.h, qtok)
                                        b.s_sp.inc(g.dma_start(out=dbg["xq"],
                                                               in_=xq_sb[:]), 16)
                                    block.gpsimd(dmp2)
                                ev_war = ([(b.s_pe.h, fst["sup_pe"])]
                                          if fst["sup_pe"] else [])
                                for c in range(DC):
                                    emit_tp(xq_sb[:, 128 * c:128 * (c + 1)],
                                            xqT[:, c, 128 * k:128 * k + 128],
                                            pe_wait=(b.s_dve.h, qtok),
                                            evac_war=ev_war)
                                fst["xq_pe"] = b.s_pe.n
                            return post
                        emit_quant(ln_dst, xq_sb[:], D, halve=False,
                                   dst_war=lambda: ((b.s_pe.h, fst["xq_pe"])
                                                    if fst.get("xq_pe") else None),
                                   post=mkpost())
                    flush_q()

                    for si, (soff, slen) in enumerate(STRIPS):
                        if first and S == 0:
                            for wdram, wstrip in ((wg_d, wgstrip), (wu_d, wustrip)):
                                for c2 in range(slen // 128):
                                    r0 = soff + 128 * c2

                                    def mkwpost(c2=c2, wstrip=wstrip):
                                        def factory(off, piece):
                                            def post(qtok):
                                                ev_war = []
                                                if fst["strip_pe"]:
                                                    ev_war.append(
                                                        (b.s_pe.h, fst["strip_pe"]))
                                                if fst.get("spill_tok"):
                                                    ev_war.append(
                                                        (b.s_sp.h, fst["spill_tok"]))
                                                for cc in range(DC):
                                                    st["gu_ev"] = emit_tp(
                                                        wq_chunk[:, 128 * cc:
                                                                 128 * (cc + 1)],
                                                        wstrip[:, cc, 128 * c2:
                                                               128 * c2 + 128],
                                                        pe_wait=(b.s_dve.h, qtok),
                                                        evac_war=ev_war)
                                                fst["wq_pe"] = b.s_pe.n
                                            return post
                                        return factory
                                    quant_w_chunk(wdram[r0:r0 + 128, :], D,
                                                  wq_chunk[:, :D], halve=False,
                                                  post=mkwpost())
                            flush_q()
                            last_ev = st["gu_ev"]

                            def spill_strips(g, soff=soff, slen=slen, ev=last_ev):
                                g.wait_ge(b.s_dve.h, ev)
                                for wspill, wstrip in ((wgq_s, wgstrip), (wuq_s, wustrip)):
                                    for cc in range(DC):
                                        b.s_sp.inc(g.dma_start(
                                            out=wspill[cc, :, soff:soff + slen],
                                            in_=wstrip[:, cc, :slen]), 16)
                            block.gpsimd(spill_strips)
                            fst["spill_tok"] = b.s_sp.n

                            def twait(t, ev=last_ev):
                                t.wait_ge(b.s_dve.h, ev)
                            block.tensor(twait)
                        else:
                            def reload_strips(s, soff=soff, slen=slen):
                                if fst["strip_pe"]:
                                    s.wait_ge(b.s_pe.h, fst["strip_pe"])
                                s.wait_ge(b.s_sp.h, b.s_sp.n)
                                for wspill, wstrip in ((wgq_s, wgstrip), (wuq_s, wustrip)):
                                    for cc in range(DC):
                                        b.s_lst.inc(s.dma_start(
                                            out=wstrip[:, cc, :slen],
                                            in_=wspill[cc, :, soff:soff + slen]), 16)
                                st["strip_re"] = b.s_lst.n
                            block.sync(reload_strips)

                            def twait2(t):
                                t.wait_ge(b.s_lst.h, st["strip_re"])
                            block.tensor(twait2)

                        def do_hjob(job):
                            par, k, slen2, soff2, tok = (job["par"], job["k"],
                                                         job["slen"], job["soff"],
                                                         job["tok"])

                            def asec(sc, par=par, slen2=slen2, tok=tok):
                                sc.wait_ge(b.s_pe.h, tok["g"])
                                if fst[f"silu_rd{par}"]:
                                    sc.wait_ge(b.s_dve.h, fst[f"silu_rd{par}"])
                                tok["silu"] = b.s_act.inc(sc.activation(
                                    silu_sb[par][:, :slen2], psG[par][:, :slen2],
                                    AF.Silu))
                            block.scalar(asec)
                            fst[f"sfree{par}"] = tok["silu"]

                            def vsec(v, par=par, slen2=slen2, tok=tok):
                                v.wait_ge(b.s_act.h, tok["silu"])
                                v.wait_ge(b.s_pe.h, tok["u"])
                                if fst.get(f"hgp{par}"):
                                    v.wait_ge(b.s_gp.h, fst[f"hgp{par}"])
                                tok["h"] = b.s_dve.inc(v.tensor_tensor(
                                    h_sb[par][:, :slen2], silu_sb[par][:, :slen2],
                                    psU[par][:, :slen2], ALU.mult))
                            block.vector(vsec)
                            fst[f"silu_rd{par}"] = tok["h"]
                            if job["dump"]:
                                def dmph(g, par=par, tok=tok):
                                    g.wait_ge(b.s_dve.h, tok["h"])
                                    b.s_sp.inc(g.dma_start(out=dbg["h"],
                                                           in_=h_sb[par][:, :]), 16)
                                    b.s_sp.inc(g.dma_start(out=dbg["up"],
                                                           in_=silu_sb[par][:, :]), 16)
                                block.gpsimd(dmph)

                            def mkhpost(par=par, k=k, soff2=soff2, slen2=slen2,
                                        dump=job["dump"]):
                                def post(qtok):
                                    fst[f"hpipe{par}"] = qtok
                                    if dump:
                                        def dmphq(g, par=par, qtok=qtok):
                                            g.wait_ge(b.s_dve.h, qtok)
                                            b.s_sp.inc(g.dma_start(
                                                out=dbg["hq"],
                                                in_=hq_sb[par][:, :]), 16)
                                        block.gpsimd(dmphq)
                                    ev_war = ([(b.s_pe.h, fst["sup_pe"])]
                                              if fst["sup_pe"] else [])
                                    for p2 in range(slen2 // 128):
                                        cc = (soff2 + 128 * p2) // 128
                                        emit_tp(hq_sb[par][:, 128 * p2:128 * p2 + 128],
                                                hqT[:, cc, 128 * k:128 * k + 128],
                                                pe_wait=(b.s_dve.h, qtok),
                                                evac_war=ev_war)
                                    fst[f"hq_pe{par}"] = b.s_pe.n
                                return post
                            qret = emit_quant(h_sb[par][:, :slen2],
                                              hq_sb[par][:, :slen2],
                                              slen2, halve=False,
                                              dst_war=lambda par=par: (
                                                  (b.s_pe.h, fst[f"hq_pe{par}"])
                                                  if fst.get(f"hq_pe{par}") else None),
                                              post=mkhpost())
                            fst[f"hgp{par}"] = qret["gp"]

                        for k, tt in enumerate(tts):
                            par = k % 2
                            tok = {}

                            def tsec(t, par=par, k=k, slen=slen, tok=tok):
                                if fst.get(f"sfree{par}"):
                                    t.wait_ge(b.s_act.h, fst[f"sfree{par}"])
                                if fst[f"silu_rd{par}"]:
                                    t.wait_ge(b.s_dve.h, fst[f"silu_rd{par}"])
                                ig = iu = None
                                for cc in range(DC):
                                    ig = t.matmul(psG[par][:, :slen],
                                                  xqT[:, cc, 128 * k:128 * k + 128],
                                                  wgstrip[:, cc, :slen],
                                                  start=(cc == 0), stop=(cc == DC - 1))
                                    iu = t.matmul(psU[par][:, :slen],
                                                  xqT[:, cc, 128 * k:128 * k + 128],
                                                  wustrip[:, cc, :slen],
                                                  start=(cc == 0), stop=(cc == DC - 1))
                                tok["g"] = b.s_pe.inc(ig)
                                tok["u"] = b.s_pe.inc(iu)
                            block.tensor(tsec)
                            fst["strip_pe"] = b.s_pe.n
                            if fst.get("hjob") is not None:
                                do_hjob(fst["hjob"])
                            fst["hjob"] = dict(
                                par=par, k=k, slen=slen, soff=soff, tok=tok,
                                dump=(DEBUG_DUMP and "ffn" in DEBUG_SEL and first
                                      and S == 0 and si == 0 and k == 2))

                    if fst.get("hjob") is not None:
                        do_hjob(fst["hjob"])
                        fst["hjob"] = None
                    flush_q()
                    for k, tt in enumerate(tts):
                        tok = {}

                        def tsec(t, k=k, tok=tok):
                            for f in range(2):
                                if fst[f"dres{f}"]:
                                    t.wait_ge(b.s_dve.h, fst[f"dres{f}"])
                                im = None
                                for cc in range(HC):
                                    im = t.matmul(psD[f][:, :],
                                                  hqT[:, cc, 128 * k:128 * k + 128],
                                                  wdqT[:, cc, 512 * f:512 * f + 512],
                                                  start=(cc == 0), stop=(cc == HC - 1))
                                tok[f"d{f}"] = b.s_pe.inc(im)
                        block.tensor(tsec)

                        def vsec(v, tt=tt, tok=tok):
                            for f in range(2):
                                v.wait_ge(b.s_pe.h, tok[f"d{f}"])
                                fst[f"dres{f}"] = b.s_dve.inc(v.tensor_add(
                                    xres[:, tt, 512 * f:512 * f + 512],
                                    xres[:, tt, 512 * f:512 * f + 512], psD[f][:, :]))
                        block.vector(vsec)
                    fst["sup_pe"] = b.s_pe.n

            barrier()

        def attention():
            actx = ExitStack()
            with actx:
                aent = actx.enter_context
                qT = aent(nc.sbuf_tensor("qT", [128, DC, T], BF16))
                kT = aent(nc.sbuf_tensor("kT", [128, DC, T], BF16))
                v_sb = aent(nc.sbuf_tensor("v_sb", [128, 16, 1040], BF16))
                ast = {"tp": 0, "evac0": 0, "evac1": 0}

                pctx = ExitStack()
                with pctx:
                    pent = pctx.enter_context
                    ln2T = pent(nc.sbuf_tensor("ln2T", [128, DC, T], BF16))
                    ln2c = pent(nc.sbuf_tensor("ln2c", [128, D], BF16))
                    wproj = pent(nc.sbuf_tensor("wproj", [128, DC, 128], BF16))
                    vtmp = pent(nc.sbuf_tensor("vtmp", [128, 512], BF16))
                    psP = [pent(nc.psum_tensor(f"psP{i}", [128, 512], F32)) for i in range(2)]
                    psTT = [pent(nc.psum_tensor(f"psTT{i}", [128, 128], BF16)) for i in range(2)]

                    def emit_tp2(src, dst, pe_wait=None, evac_war=None):
                        i = ast["tp"] % 2
                        ast["tp"] += 1
                        tk = {}

                        def tsec(t):
                            if pe_wait is not None:
                                t.wait_ge(pe_wait[0], pe_wait[1])
                            if ast[f"evac{i}"]:
                                t.wait_ge(b.s_dve.h, ast[f"evac{i}"])
                            tk["tp"] = b.s_pe.inc(t.transpose(psTT[i][:], src, idn[:]))
                        block.tensor(tsec)

                        def vsec(v):
                            v.wait_ge(b.s_pe.h, tk["tp"])
                            if evac_war is not None:
                                v.wait_ge(evac_war[0], evac_war[1])
                            tk["ev"] = b.s_dve.inc(v.tensor_copy(dst, psTT[i][:]))
                        block.vector(vsec)
                        ast[f"evac{i}"] = tk["ev"]
                        return tk["ev"]

                    lamtok = {}

                    def lam_v(v):
                        t = b.s_dve.inc(v.tensor_mul(rowscr[0:1, 128:160],
                                                     rowscr[0:1, 0:32],
                                                     rowscr[0:1, 32:64]))
                        v.wait_ge(b.s_dve.h, t)
                        t = b.s_dve.inc(v.tensor_mul(rowscr[0:1, 160:192],
                                                     rowscr[0:1, 64:96],
                                                     rowscr[0:1, 96:128]))
                        v.wait_ge(b.s_dve.h, t)
                        t = b.s_dve.inc(v.reduce_sum(rowscr[0:1, 192:193],
                                                     rowscr[0:1, 128:160], AX.X))
                        v.wait_ge(b.s_dve.h, t)
                        t = b.s_dve.inc(v.reduce_sum(rowscr[0:1, 193:194],
                                                     rowscr[0:1, 160:192], AX.X))
                        v.wait_ge(b.s_dve.h, t)
                        lamtok["sums"] = t
                    block.vector(lam_v)

                    def lam_a(sc):
                        sc.wait_ge(b.s_dve.h, lamtok["sums"])
                        lamtok["exps"] = b.s_act.inc(sc.activation(
                            rowscr[0:1, 194:196], rowscr[0:1, 192:194], AF.Exp))
                    block.scalar(lam_a)

                    def lam_v2(v):
                        v.wait_ge(b.s_act.h, lamtok["exps"])
                        t = b.s_dve.inc(v.tensor_sub(rowscr[0:1, 196:197],
                                                     rowscr[0:1, 194:195],
                                                     rowscr[0:1, 195:196]))
                        v.wait_ge(b.s_dve.h, t)
                        lamtok["lam"] = b.s_dve.inc(v.tensor_single_scalar(
                            rowscr[0:1, 196:197], rowscr[0:1, 196:197], LAM0, ALU.add))
                        v.wait_ge(b.s_dve.h, lamtok["lam"])
                    block.vector(lam_v2)

                    def lam_t(t):
                        t.wait_ge(b.s_dve.h, lamtok["lam"])
                        lamtok["bc"] = b.s_pe.inc(t.matmul(
                            psP[0][:, 0:1], ones[0:1, :], rowscr[0:1, 196:197],
                            start=True, stop=True))
                    block.tensor(lam_t)

                    def lam_v3(v):
                        v.wait_ge(b.s_pe.h, lamtok["bc"])
                        t = b.s_dve.inc(v.tensor_copy(lamcol[:], psP[0][:, 0:1]))
                        v.wait_ge(b.s_dve.h, t)
                    block.vector(lam_v3)

                    for tt in range(TC if ATTN_STAGE >= 2 else 0):
                        war = (b.s_pe.h, ast.get("ln2c_pe", 0)) if ast.get("ln2c_pe") else None
                        lntok2 = emit_ln(xres[:, tt, :], ln2c[:], dst_war=war)
                        for c in range(DC):
                            emit_tp2(ln2c[:, 128 * c:128 * (c + 1)],
                                     ln2T[:, c, 128 * tt:128 * tt + 128],
                                     pe_wait=(b.s_dve.h, lntok2))
                        ast["ln2c_pe"] = b.s_pe.n
                    ast["ln2T_ev"] = b.s_dve.n

                    def project_fm(w_dram, dstT):
                        for fo in range(DC):
                            ctok = {}

                            def gsec(g, fo=fo, ctok=ctok):
                                if ast.get("wproj_pe"):
                                    g.wait_ge(b.s_pe.h, ast["wproj_pe"])
                                for e in range(DC):
                                    ctok[e] = b.s_cast.inc(g.dma_start(
                                        out=wproj[:, e, :],
                                        in_=w_dram[128 * e:128 * (e + 1),
                                                   128 * fo:128 * (fo + 1)]), 16)
                            block.gpsimd(gsec)

                            for ts4 in range(4):
                                par = ts4 % 2
                                mtok = {}

                                def tsec(t, ts4=ts4, par=par, ctok=ctok, mtok=mtok):
                                    t.wait_ge(b.s_cast.h, ctok[DC - 1])
                                    t.wait_ge(b.s_dve.h, ast["ln2T_ev"])
                                    if ast.get(f"pevac{par}"):
                                        t.wait_ge(b.s_act.h, ast[f"pevac{par}"])
                                    im = None
                                    for e in range(DC):
                                        im = t.matmul(psP[par][:], wproj[:, e, :],
                                                      ln2T[:, e, 512 * ts4:512 * ts4 + 512],
                                                      start=(e == 0), stop=(e == DC - 1))
                                    mtok["m"] = b.s_pe.inc(im)
                                block.tensor(tsec)
                                ast["wproj_pe"] = b.s_pe.n

                                def asecE(sc, fo=fo, ts4=ts4, par=par, mtok=mtok, dstT=dstT):
                                    sc.wait_ge(b.s_pe.h, mtok["m"])
                                    ast[f"pevac{par}"] = b.s_act.inc(sc.copy(
                                        dstT[:, fo, 512 * ts4:512 * ts4 + 512],
                                        psP[par][:]))
                                block.scalar(asecE)

                    if ATTN_STAGE >= 3:
                        project_fm(wq_d, qT)
                        project_fm(wk_d, kT)

                    def vv(v):
                        t = b.s_dve.inc(v.memset(
                            v_sb[:].rearrange("p s (hh c) -> p s hh c", c=65)[:, :, :, 64:65],
                            1.0))
                        v.wait_ge(b.s_dve.h, t)
                    block.vector(vv)

                    for dc_i in range(DC if ATTN_STAGE >= 4 else 0):
                        ctok = {}

                        def gsec(g, dc_i=dc_i, ctok=ctok):
                            if ast.get("wproj_pe"):
                                g.wait_ge(b.s_pe.h, ast["wproj_pe"])
                            for e in range(DC):
                                ctok[e] = b.s_cast.inc(g.dma_start(
                                    out=wproj[:, e, :],
                                    in_=wv_d[128 * e:128 * (e + 1),
                                             128 * dc_i:128 * (dc_i + 1)]), 16)
                        block.gpsimd(gsec)

                        for ts4 in range(4):
                            par = ts4 % 2
                            mtok = {}

                            def tsec(t, ts4=ts4, par=par, ctok=ctok, mtok=mtok):
                                t.wait_ge(b.s_cast.h, ctok[DC - 1])
                                t.wait_ge(b.s_dve.h, ast["ln2T_ev"])
                                if ast.get(f"pevac{par}"):
                                    t.wait_ge(b.s_act.h, ast[f"pevac{par}"])
                                im = None
                                for e in range(DC):
                                    im = t.matmul(psP[par][:], wproj[:, e, :],
                                                  ln2T[:, e, 512 * ts4:512 * ts4 + 512],
                                                  start=(e == 0), stop=(e == DC - 1))
                                mtok["m"] = b.s_pe.inc(im)
                            block.tensor(tsec)
                            ast["wproj_pe"] = b.s_pe.n

                            def asecE(sc, par=par, mtok=mtok):
                                sc.wait_ge(b.s_pe.h, mtok["m"])
                                if ast.get("vtmp_pe"):
                                    sc.wait_ge(b.s_pe.h, ast["vtmp_pe"])
                                mtok["ev"] = b.s_act.inc(sc.copy(vtmp[:], psP[par][:]))
                            block.scalar(asecE)
                            ast[f"pevac{par}"] = mtok["ev"]

                            for p2 in range(4):
                                sc_i = 4 * ts4 + p2
                                dstv = v_sb[:, sc_i, :].rearrange(
                                    "p (hh c) -> p hh c", c=65)[:, 2 * dc_i:2 * dc_i + 2, 0:64]
                                emit_tp2(vtmp[:, 128 * p2:128 * p2 + 128], dstv,
                                         pe_wait=(b.s_act.h, mtok["ev"]))
                            ast["vtmp_pe"] = b.s_pe.n

                    if DEBUG_DUMP and "proj" in DEBUG_SEL:
                        def dmpp(g_):
                            g_.wait_ge(b.s_dve.h, b.s_dve.n)
                            b.s_sp.inc(g_.dma_start(out=dbg["qt0"], in_=qT[:, 0, 0:512]), 16)
                            b.s_sp.inc(g_.dma_start(out=dbg["ln2t0"], in_=ln2T[:, 0, 0:512]), 16)
                            b.s_sp.inc(g_.dma_start(out=dbg["kt0"], in_=kT[:, 0, :]), 16)
                            b.s_sp.inc(g_.dma_start(out=dbg["v0"], in_=v_sb[:, 0, :]), 16)
                        block.gpsimd(dmpp)
                    barrier()

                sctx = ExitStack()
                with sctx:
                    sent = sctx.enter_context
                    e_sb = sent(nc.sbuf_tensor("e_sb", [128, 4, 2, 512], BF16))
                    nrm = sent(nc.sbuf_tensor("nrm", [128, 16, 512], BF16))
                    scr_o = sent(nc.sbuf_tensor("scr_o", [128, 512], F32))
                    scr_t = sent(nc.sbuf_tensor("scr_t", [128, 512], F32))
                    scr_cr = sent(nc.sbuf_tensor("scr_cr", [128, 512], F32))
                    wob = sent(nc.sbuf_tensor("wob", [128, 4, 512], BF16))
                    wob2 = sent(nc.sbuf_tensor("wob2", [128, 4, 512], BF16))
                    psSc = [sent(nc.psum_tensor(f"psSc{i}", [128, 512], F32)) for i in range(4)]
                    psA = [sent(nc.psum_tensor(f"psA{i}", [128, 512], F32)) for i in range(4)]

                    sst = {}
                    for i in range(4):
                        sst[f"exp{i}"] = 0
                        sst[f"av{i}0"] = 0
                        sst[f"av{i}1"] = 0
                        sst[f"scfree{i}"] = None
                        sst[f"pafree{i}"] = None
                    sst["wo_pe"] = 0
                    sst["nrm_dve"] = 0

                    def guard_v(v, inst):
                        t = b.s_dve.inc(inst)
                        v.wait_ge(b.s_dve.h, t)
                        return t

                    def guard_a(sc, inst):
                        t = b.s_act.inc(inst)
                        sc.wait_ge(b.s_act.h, t)
                        return t

                    def do_avjob(job):
                        def tv(t, job=job):
                            g2, s_i, par = job["g"], job["s"], job["par"]
                            for i in range(4):
                                t.wait_ge(b.s_act.h, job["exp"][i])
                                if s_i == 0 and sst[f"pafree{i}"] is not None:
                                    w = sst[f"pafree{i}"]
                                    t.wait_ge(w[0], w[1])
                                    sst[f"pafree{i}"] = None
                                vj = 2 * g2 + i // 2
                                im = t.matmul(
                                    psA[i][0:65, :],
                                    v_sb[:, s_i, 65 * vj:65 * vj + 65],
                                    e_sb[:, i, job["par"], :],
                                    start=(s_i == 0), stop=(s_i == ATTN_NS - 1))
                                sst[f"av{i}{par}"] = b.s_pe.inc(im)
                                job["avtok"][i] = sst[f"av{i}{par}"]
                        block.tensor(tv)

                    for tau in range(ATTN_NTAU if ATTN_STAGE >= 5 else 0):
                        for g in range(ATTN_NG):
                            avpend = [None]
                            for s_i in range(ATTN_NS):
                                par = s_i % 2
                                qktok = {}

                                def tq(t, g=g, tau=tau, s_i=s_i, qktok=qktok,
                                       prev=avpend[0]):
                                    if ATTN_SUB == 5:
                                        for i in range(4):
                                            if sst[f"scfree{i}"] is not None:
                                                w = sst[f"scfree{i}"]
                                                t.wait_ge(w[0], w[1])
                                                sst[f"scfree{i}"] = None
                                        for i in range(4):
                                            if i == 2:
                                                t.wait_ge(b.s_pe.h, qktok[1])
                                            t.ldweights(
                                                kT[32*i:32*i+32, g, 128*s_i:128*s_i+128],
                                                tile_position=(32*i, 0))
                                            qktok[i] = b.s_pe.inc(t.matmul(
                                                psSc[i][:],
                                                kT[32*i:32*i+32, g, 128*s_i:128*s_i+128],
                                                qT[32*i:32*i+32, g, 512*tau:512*tau+512],
                                                start=True, stop=True,
                                                tile_position=(32*i, 0)))
                                        return
                                    if ATTN_SUB == -1:
                                        for i in range(3):
                                            t.ldweights(kT[32*i:32*i+32, g, 128*s_i:128*s_i+128],
                                                        tile_position=(32*i, 0))
                                            qktok[i] = b.s_pe.inc(t.matmul(
                                                psSc[i][:],
                                                kT[32*i:32*i+32, g, 128*s_i:128*s_i+128],
                                                qT[32*i:32*i+32, g, 512*tau:512*tau+512],
                                                start=True, stop=True,
                                                tile_position=(32*i, 0)))
                                        qktok[3] = qktok[2]
                                        return
                                    if ATTN_SUB == 0:
                                        qktok[0] = b.s_pe.inc(t.matmul(
                                            psSc[0][:], kT[:, 0, 0:128], kT[:, 0, 0:512],
                                            start=True, stop=True))
                                        for i in range(1, 4):
                                            qktok[i] = qktok[0]
                                        return
                                    if ATTN_SUB == -2:
                                        qktok[0] = b.s_pe.inc(t.matmul(
                                            psSc[0][:], kT[0:32, 0, 0:128], kT[0:32, 0, 0:512],
                                            start=True, stop=True))
                                        for i in range(1, 4):
                                            qktok[i] = qktok[0]
                                        return
                                    if ATTN_SUB == -4:
                                        for i in range(2):
                                            t.ldweights(kT[32*i:32*i+32, 0, 0:128],
                                                        tile_position=(32*i, 0))
                                            qktok[i] = b.s_pe.inc(t.matmul(
                                                psSc[i][:], kT[32*i:32*i+32, 0, 0:128],
                                                kT[32*i:32*i+32, 0, 0:512],
                                                start=True, stop=True,
                                                tile_position=(32*i, 0)))
                                        qktok[2] = qktok[1]; qktok[3] = qktok[1]
                                        return
                                    if ATTN_SUB == -3:
                                        t.ldweights(kT[32:64, 0, 0:128], tile_position=(32, 0))
                                        qktok[0] = b.s_pe.inc(t.matmul(
                                            psSc[0][:], kT[32:64, 0, 0:128], kT[32:64, 0, 0:512],
                                            start=True, stop=True, tile_position=(32, 0)))
                                        for i in range(1, 4):
                                            qktok[i] = qktok[0]
                                        return
                                    for i in range(4):
                                        if sst[f"scfree{i}"] is not None:
                                            w = sst[f"scfree{i}"]
                                            t.wait_ge(w[0], w[1])
                                            sst[f"scfree{i}"] = None
                                        if prev is not None and i in (0, 2):
                                            t.wait_ge(b.s_act.h,
                                                      prev["exp"][i + 1])
                                        if i == 2:
                                            t.wait_ge(b.s_pe.h, qktok[1])
                                        t.ldweights(
                                            kT[32 * i:32 * i + 32, g, 128 * s_i:128 * s_i + 128],
                                            tile_position=(32 * i, 0))
                                        qktok[i] = b.s_pe.inc(t.matmul(
                                            psSc[i][:],
                                            kT[32 * i:32 * i + 32, g, 128 * s_i:128 * s_i + 128],
                                            qT[32 * i:32 * i + 32, g, 512 * tau:512 * tau + 512],
                                            start=True, stop=True,
                                            tile_position=(32 * i, 0)))
                                block.tensor(tq)
                                if ATTN_SUB < 2:
                                    continue

                                job = {"g": g, "s": s_i, "par": par, "exp": {},
                                       "avtok": {}}

                                def ta(sc, g=g, s_i=s_i, par=par, qktok=qktok,
                                       job=job):
                                    for i in range(4):
                                        sc.wait_ge(b.s_pe.h, qktok[i])
                                        if sst[f"av{i}{par}"]:
                                            sc.wait_ge(b.s_pe.h, sst[f"av{i}{par}"])
                                        sst[f"exp{i}"] = b.s_act.inc(sc.activation(
                                            e_sb[:, i, par, :], psSc[i][:], AF.Exp,
                                            scale=SCAL))
                                        job["exp"][i] = sst[f"exp{i}"]
                                block.scalar(ta)
                                if ATTN_SUB < 3:
                                    continue

                                if avpend[0] is not None:
                                    do_avjob(avpend[0])
                                avpend[0] = job
                            if avpend[0] is not None:
                                do_avjob(avpend[0])
                                avtok = avpend[0]["avtok"]
                                avpend[0] = None

                            for jj in range(2 if ATTN_STAGE >= 6 else 0):
                                A0 = psA[2 * jj]
                                A1 = psA[2 * jj + 1]
                                jg = 2 * g + jj
                                ep = {}
                                rs0 = rowscr[64:65, 0:512]
                                rs1 = rowscr[64:65, 512:1024]
                                rb0 = rowscr[0:1, 0:512]
                                rb1 = rowscr[0:1, 512:1024]

                                def ev1(v, A0=A0, A1=A1, ep=ep):
                                    v.wait_ge(b.s_pe.h, ep["avmax"])
                                    if ep.get("rowscr_pe"):
                                        v.wait_ge(b.s_pe.h, ep["rowscr_pe"])
                                    guard_v(v, v.tensor_copy(rs0, A0[64:65, :]))
                                    guard_v(v, v.tensor_copy(rs1, A1[64:65, :]))
                                    guard_v(v, v.reciprocal(rs1, rs1))
                                    guard_v(v, v.tensor_mul(rs1, rs0, rs1))
                                    ep["crow"] = guard_v(v, v.tensor_scalar(
                                        rs1, rs1, lamcol[64:65, 0:1], None, ALU.mult))
                                ep["avmax"] = max(avtok.values())
                                ep["rowscr_pe"] = sst.get("row_pe", 0)
                                block.vector(ev1)

                                def et1(t, ep=ep, jj=jj):
                                    t.wait_ge(b.s_dve.h, ep["crow"])
                                    ep["mmC"] = b.s_pe.inc(t.matmul(
                                        psSc[0][0:64, :], ones[64:65, 0:64], rs1,
                                        start=True, stop=True, tile_position=(64, 0)))
                                    ep["mmE"] = b.s_pe.inc(t.matmul(
                                        psSc[1][0:1, :], ones[64:65, 0:1], rs0,
                                        start=True, stop=True, tile_position=(64, 0)))
                                block.tensor(et1)
                                sst["row_pe"] = ep["mmE"]

                                def ev2(v, A0=A0, A1=A1, ep=ep):
                                    v.wait_ge(b.s_pe.h, ep["mmC"])
                                    ep["c6"] = guard_v(v, v.tensor_copy(scr_cr[0:64, :],
                                                                        psSc[0][0:64, :]))
                                    v.tensor_mul(scr_t[0:64, :], A1[0:64, :], scr_cr[0:64, :])
                                    ep["o"] = guard_v(v, v.tensor_sub(
                                        scr_o[0:64, :], A0[0:64, :], scr_t[0:64, :]))
                                    ep["sq"] = guard_v(v, v.tensor_mul(
                                        scr_t[0:64, :], scr_o[0:64, :], scr_o[0:64, :]))
                                    v.wait_ge(b.s_pe.h, ep["mmE"])
                                    ep["er"] = guard_v(v, v.tensor_copy(rb0, psSc[1][0:1, :]))
                                    guard_v(v, v.tensor_mul(rb1, rb0, rb0))
                                    ep["ersq"] = guard_v(v, v.tensor_single_scalar(
                                        rb1, rb1, 1e-5, ALU.mult))
                                block.vector(ev2)

                                def et2(t, ep=ep):
                                    t.wait_ge(b.s_dve.h, ep["sq"])
                                    ep["mmS"] = b.s_pe.inc(t.matmul(
                                        psSc[2][0:1, :], ones[0:64, 0:1], scr_t[0:64, :],
                                        start=True, stop=True))
                                block.tensor(et2)

                                def ev3(v, ep=ep):
                                    v.wait_ge(b.s_pe.h, ep["mmS"])
                                    ep["u2"] = guard_v(v, v.scalar_tensor_tensor(
                                        rb0, psSc[2][0:1, :], 1.0 / 64.0, rb1,
                                        ALU.mult, ALU.add))
                                block.vector(ev3)

                                def ea1(sc, ep=ep):
                                    sc.wait_ge(b.s_dve.h, ep["u2"])
                                    guard_a(sc, sc.activation(rb1, rb0, AF.Ln))
                                    ep["rstd"] = guard_a(sc, sc.activation(
                                        rb1, rb1, AF.Exp, scale=-0.5))
                                block.scalar(ea1)

                                def et3(t, ep=ep):
                                    t.wait_ge(b.s_act.h, ep["rstd"])
                                    ep["mmR"] = b.s_pe.inc(t.matmul(
                                        psSc[3][0:64, :], ones[0:1, 0:64], rb1,
                                        start=True, stop=True))
                                block.tensor(et3)
                                sst["row_pe"] = ep["mmR"]

                                def ev4(v, ep=ep, jg=jg):
                                    v.wait_ge(b.s_pe.h, ep["mmR"])
                                    ep["c14"] = guard_v(v, v.tensor_copy(
                                        scr_cr[0:64, :], psSc[3][0:64, :]))
                                    v.tensor_mul(scr_t[0:64, :], scr_o[0:64, :],
                                                 scr_cr[0:64, :])
                                    if sst["wo_pe"]:
                                        v.wait_ge(b.s_pe.h, sst["wo_pe"])
                                    ep["nrm"] = b.s_dve.inc(v.tensor_scalar(
                                        nrm[0:64, jg, :], scr_t[0:64, :],
                                        sublncol[0:64, 0:1], None, ALU.mult))
                                block.vector(ev4)
                                sst["nrm_dve"] = ep["nrm"]
                                if DEBUG_DUMP and "epi" in DEBUG_SEL and tau == 0 and g == 0 and jj == 0:
                                    def dmp(g_, ep=ep):
                                        g_.wait_ge(b.s_dve.h, ep["nrm"])
                                        b.s_sp.inc(g_.dma_start(out=dbg["lamcol"], in_=lamcol[:]), 16)
                                        b.s_sp.inc(g_.dma_start(out=dbg["rowscr"], in_=rowscr[:]), 16)
                                        b.s_sp.inc(g_.dma_start(out=dbg["scro"], in_=scr_o[:]), 16)
                                        b.s_sp.inc(g_.dma_start(out=dbg["nrm0"], in_=nrm[:, 0, :]), 16)
                                        b.s_sp.inc(g_.dma_start(out=dbg["e0"], in_=e_sb[:, 0, 1, :]), 16)
                                    block.gpsimd(dmp)

                                sst[f"pafree{2 * jj}"] = (b.s_dve.h, ep["o"])
                                sst[f"pafree{2 * jj + 1}"] = (b.s_dve.h, ep["o"])
                                sst["scfree0"] = (b.s_dve.h, ep["c6"])
                                sst["scfree1"] = (b.s_dve.h, ep["er"])
                                sst["scfree2"] = (b.s_dve.h, ep["u2"])
                                sst["scfree3"] = (b.s_dve.h, ep["c14"])

                        if DEBUG_DUMP and "nrm" in DEBUG_SEL and tau == 0:
                            def dmpn(g_):
                                g_.wait_ge(b.s_dve.h, sst["nrm_dve"])
                                b.s_sp.inc(g_.dma_start(out=dbg["nrmall"], in_=nrm[:]), 16)
                            block.gpsimd(dmpn)
                        for fo in range(2 if ATTN_STAGE >= 7 else 0):
                            for cc in range(8):
                                par = cc % 4
                                wtok = {}

                                castsem = b.s_cast if par % 2 == 0 else b.s_cast1

                                def gsec(g_, fo=fo, cc=cc, par=par, wtok=wtok,
                                         castsem=castsem):
                                    if sst.get(f"wob_pe{par}"):
                                        g_.wait_ge(b.s_pe.h, sst[f"wob_pe{par}"])
                                    castsem.inc(g_.dma_start(
                                        out=wob[0:64, par, :],
                                        in_=wo_d[128 * cc:128 * cc + 64,
                                                 512 * fo:512 * fo + 512]), 16)
                                    wtok["ld"] = castsem.inc(g_.dma_start(
                                        out=wob2[0:64, par, :],
                                        in_=wo_d[128 * cc + 64:128 * cc + 128,
                                                 512 * fo:512 * fo + 512]), 16)
                                block.gpsimd(gsec)

                                def tsec(t, fo=fo, cc=cc, par=par, wtok=wtok, tau=tau,
                                         castsem=castsem):
                                    t.wait_ge(castsem.h, wtok["ld"])
                                    if cc == 0:
                                        t.wait_ge(b.s_dve.h, sst["nrm_dve"])
                                        for tt in range(4):
                                            w = sst[f"pafree{tt}"]
                                            if w is not None:
                                                t.wait_ge(w[0], w[1])
                                                sst[f"pafree{tt}"] = None
                                            if sst.get(f"wores{tt}"):
                                                t.wait_ge(b.s_dve.h, sst[f"wores{tt}"])
                                    im = None
                                    for tt in range(4):
                                        im = t.matmul(
                                            psA[tt][:],
                                            nrm[0:64, 2 * cc, 128 * tt:128 * tt + 128],
                                            wob[0:64, par, :],
                                            start=(cc == 0), stop=False)
                                        last = (cc == 7)
                                        im = t.matmul(
                                            psA[tt][:],
                                            nrm[0:64, 2 * cc + 1, 128 * tt:128 * tt + 128],
                                            wob2[0:64, par, :],
                                            start=False, stop=last)
                                        wtok[f"m{tt}"] = b.s_pe.inc(im)
                                    sst[f"wob_pe{par}"] = wtok["m3"]
                                block.tensor(tsec)

                                if cc == 7:
                                    def vres(v, fo=fo, tau=tau, wtok=wtok):
                                        for tt in range(4):
                                            v.wait_ge(b.s_pe.h, wtok[f"m{tt}"])
                                            gt = 4 * tau + tt
                                            sst[f"wores{tt}"] = b.s_dve.inc(v.tensor_add(
                                                xres[:, gt, 512 * fo:512 * fo + 512],
                                                xres[:, gt, 512 * fo:512 * fo + 512],
                                                psA[tt][:]))
                                    block.vector(vres)
                        sst["wo_pe"] = b.s_pe.n
                        for tt in range(4):
                            if sst.get(f"wores{tt}"):
                                sst[f"pafree{tt}"] = (b.s_dve.h, sst[f"wores{tt}"])

                    barrier()

        ffn(first=True)
        if n_phases >= 2:
            attention()
        if n_phases >= 3:
            ffn(first=False)

        barrier()

        def store_out(s):
            for t in range(TC):
                b.s_st.inc(s.dma_start(out=out_d[128 * t:128 * (t + 1), :],
                                       in_=xres[:, t, :]), 16)
            s.wait_ge(b.s_st.h, b.s_st.n)
        block.sync(store_out)

    return nc


def kernel(L0_attn_query, L0_attn_Wq, L0_attn_Wk, L0_attn_Wv, L0_attn_Wo,
           L0_attn_lambda_q1, L0_attn_lambda_k1, L0_attn_lambda_q2,
           L0_attn_lambda_k2, L0_attn_subln_w, L0_ffn_w_gate, L0_ffn_w_up,
           L0_ffn_w_down):
    x = np.ascontiguousarray(np.asarray(L0_attn_query, dtype=np.float32))
    nc = build_program(N_PHASES)
    lam4 = np.concatenate([np.asarray(L0_attn_lambda_q1, np.float32).ravel(),
                           np.asarray(L0_attn_lambda_k1, np.float32).ravel(),
                           np.asarray(L0_attn_lambda_q2, np.float32).ravel(),
                           np.asarray(L0_attn_lambda_k2, np.float32).ravel()])
    common = {
        "wq": np.ascontiguousarray(np.asarray(L0_attn_Wq, np.float32).T),
        "wk": np.ascontiguousarray(np.asarray(L0_attn_Wk, np.float32).T),
        "wv": np.ascontiguousarray(np.asarray(L0_attn_Wv, np.float32).T),
        "wo": np.ascontiguousarray(np.asarray(L0_attn_Wo, np.float32).T),
        "wg": np.ascontiguousarray(np.asarray(L0_ffn_w_gate, np.float32)),
        "wu": np.ascontiguousarray(np.asarray(L0_ffn_w_up, np.float32)),
        "wd": np.ascontiguousarray(np.asarray(L0_ffn_w_down, np.float32)),
        "lam4": np.ascontiguousarray(lam4.reshape(1, 128)),
        "subln": np.asarray(L0_attn_subln_w, np.float32).reshape(64, 1).copy(),
        "idn": np.eye(128, dtype=BF),
        "onesrow": np.ones((128, 128), np.float32),
    }
    in_maps = [dict(common, x=np.ascontiguousarray(x[c])) for c in range(8)]
    res = run_bass_kernel_spmd(nc, in_maps, core_ids=list(range(8)))
    out = np.stack([np.asarray(res.results[c]["out"]) for c in range(8)], axis=0)
    return out.astype(np.float32)
